# version 10
# baseline (speedup 1.0000x reference)
import contextlib
from collections import defaultdict

import concourse.bass as bass
import concourse.mybir as mybir

F32 = mybir.dt.float32
BF16 = mybir.dt.bfloat16
I32 = mybir.dt.int32
AF = mybir.ActivationFunctionType
ALU = mybir.AluOpType
AX = mybir.AxisListType

T = 4096
D = 1024
NT = 32
DC = 8
ENT = 17
ETK = ENT * 128
H = 8
LAT = 64
HID = 4096
NHC = 32
EPS = 1e-5
SCALE = 1.0 / 16.0
ESH = -3.0
QCH = [(0, 512), (512, 512), (1024, 512), (1536, 512), (2048, 128)]
SEG = [(0, 3), (3, 3), (6, 3), (9, 3), (12, 3), (15, 2)]


class MK:
    def __init__(self, debug=None):
        self.debug = debug
        self.nc = bass.Bass("TRN2", target_bir_lowering=False, detect_race_conditions=False)
        self.cnt = defaultdict(int)
        self.didx = defaultdict(int)
        self.dcnt = {b: [0, 0, 0, 0] for b in ("LD", "GD", "SD", "ST")}
        self.sems = {}
        self.stack = contextlib.ExitStack()
        self._open_stacks = []
        self.dbg_outs = []

    def sem(self, name):
        self.sems[name] = self.stack.enter_context(self.nc.semaphore(name))

    def dinc(self, inst, base, n=16):
        k = self.didx[base] % 4
        self.didx[base] += 1
        inst.then_inc(self.sems[base + str(k)], n)
        self.dcnt[base][k] += n

    def dsnap(self, base):
        return tuple(self.dcnt[base])

    def dwait(self, eng, base, snap=None):
        snap = tuple(self.dcnt[base]) if snap is None else snap
        for k in range(4):
            if snap[k] > 0:
                eng.wait_ge(self.sems[base + str(k)], snap[k])

    def inc(self, inst, name, n=1):
        inst.then_inc(self.sems[name], n)
        self.cnt[name] += n

    def w(self, eng, name, val=None):
        v = self.cnt[name] if val is None else val
        if v > 0:
            eng.wait_ge(self.sems[name], v)

    def sb(self, name, shape, dt=F32, stack=None):
        return (stack or self.stack).enter_context(self.nc.sbuf_tensor(name, shape, dt))

    def psum(self, name, shape, dt=F32, stack=None):
        return (stack or self.stack).enter_context(self.nc.psum_tensor(name, shape, dt))

    def emit_round_ladder(self, vector, t_ap, ni_ap, nf_ap, v_ap, out_ap, off):
        vector.tensor_copy(out=ni_ap, in_=t_ap)
        vector.tensor_copy(out=nf_ap, in_=ni_ap)
        vector.tensor_tensor(out=v_ap, in0=t_ap, in1=nf_ap, op=ALU.subtract)
        vector.tensor_scalar(out=out_ap, in0=nf_ap, scalar1=float(off - 3),
                             scalar2=None, op0=ALU.add)
        for c in (-2.5, -1.5, -0.5, 0.5, 1.5, 2.5):
            i = vector.scalar_tensor_tensor(out=out_ap, in0=v_ap, scalar=c,
                                            in1=out_ap, op0=ALU.is_ge, op1=ALU.add)
        return i

    def chain(self, inst, eng, name):
        self.inc(inst, name)
        self.w(eng, name)

    def gate(self, eng, dma=True):
        for nm in ("TE", "VE", "SE", "GE"):
            self.w(eng, nm)
        if dma:
            for b in ("LD", "GD", "SD", "ST"):
                self.dwait(eng, b)

    def dbg_store(self, pairs):
        block = self.block
        w, inc = self.w, self.inc
        dinc = self.dinc
        @block.sync
        def _(sync):
            for nm in ("VE", "SE", "TE", "GE"):
                w(sync, nm)
            for dst, src in pairs:
                dinc(sync.dma_start(out=dst, in_=src), "ST")
            self.dwait(sync, "ST")

    def stats_v(self, vector, src_ap0, src_ap1, c):
        sA, stats = self.sA, self.stats
        vector.bn_stats(out=stats[:, 0:6], in_=src_ap0)
        self.chain(vector.bn_stats(out=stats[:, 6:12], in_=src_ap1), vector, "VE")
        self.chain(vector.bn_aggr(out=sA[:, c:c + 2], in_=stats[:, :]), vector, "VE")
        self.chain(vector.tensor_scalar(out=sA[:, c + 1:c + 2], in0=sA[:, c + 1:c + 2],
                                        scalar1=EPS, scalar2=None, op0=ALU.add), vector, "VE")
        return vector.reciprocal(out=sA[:, c + 1:c + 2], in_=sA[:, c + 1:c + 2])

    def stats_s(self, scalar, c):
        return scalar.activation(out=self.sA[:, c + 1:c + 2], in_=self.sA[:, c + 1:c + 2],
                                 func=AF.Sqrt)

    def build(self):
        nc = self.nc
        debug = self.debug
        din = lambda name, shape, dt=F32: nc.dram_tensor(name, shape, dt, kind="ExternalInput").ap()

        s = self
        s.xb = din("xb", [T, D])
        s.jsel = din("jsel", [1, 2])
        s.identf = din("identf", [128, 128], F32)
        s.identb = din("identb", [128, 128], BF16)
        s.WiT = din("WiT", [D, D])
        s.WrT = din("WrT", [D, D])
        s.bi_pp = din("bi_pp", [128, DC])
        s.br_pp = din("br_pp", [128, DC])
        s.lam = din("lam", [1, 1])
        s.Win = din("Win", [D, 64])
        s.binr = din("binr", [1, 64])
        s.W1r = din("W1r", [1, 1024])
        s.b1r = din("b1r", [1, 1024])
        s.W2r = din("W2r", [1, 1024])
        s.b2r = din("b2r", [1, 64])
        s.Wout = din("Wout", [64, D])
        s.boutr = din("boutr", [1, D])
        s.Wq = din("Wq", [2048, D])
        s.Wk_up = din("Wk_up", [2048, LAT])
        s.Wv_up = din("Wv_up", [2048, LAT])
        s.WoT = din("WoT", [2048, D])
        s.WkvdT = din("WkvdT", [D, LAT])
        s.w1T = din("w1T", [D, HID])
        s.w2T = din("w2T", [HID, D])
        s.cwr = din("cwr", [128, 3 * NHC])
        s.cbr = din("cbr", [128, NHC])

        s.y = nc.dram_tensor("y", [ETK, D], F32, kind="ExternalOutput").ap()
        s.scr_w1 = nc.dram_tensor("scr_w1", [NHC * DC, 128 * 128], BF16).ap()
        s.scr_w2 = nc.dram_tensor("scr_w2", [HID, D], BF16).ap()
        s.scr_x2 = nc.dram_tensor("scr_x2", [T, D], F32).ap()

        for nm in ("TE", "VE", "SE", "GE"):
            self.sem(nm)
        for b in ("LD", "GD", "SD", "ST"):
            for k in range(4):
                self.sem(b + str(k))

        sb = self.sb
        s.idf = sb("idf", [128, 128], F32)
        s.idb = sb("idb", [128, 128], BF16)
        s.onesf1 = sb("onesf1", [1, 128], F32)
        s.onescol = sb("onescol", [128, 1], F32)
        s.jv = sb("jv", [128, 2], F32)
        s.c8v = sb("c8v", [128, 1], F32)
        s.eshv = sb("eshv", [128, 1], F32)
        s.cw_g = sb("cw_g", [128, 3 * NHC], F32)
        s.cb_g = sb("cb_g", [128, NHC], F32)
        s.kvdb = sb("kvdb", [128, DC * 128], BF16)
        s.wneg = sb("wneg", [128, 2], F32)
        s.wtau = sb("wtau", [128, 4], F32)
        s.t2 = sb("t2", [1, 16], F32)
        s.stats = sb("stats", [128, 12], F32)
        s.sA = sb("sA", [128, 16], F32)
        st_l0c = contextlib.ExitStack()
        self._open_stacks.append(st_l0c)
        self._st_l0c = st_l0c
        s.bi_s = sb("bi_s", [128, DC], F32, stack=st_l0c)
        s.br_s = sb("br_s", [128, DC], F32, stack=st_l0c)
        s.winb = sb("winb", [128, DC * 64], BF16, stack=st_l0c)
        s.cswb = sb("cswb", [128, 64], F32, stack=st_l0c)
        s.binb = sb("binb", [128, 64], BF16, stack=st_l0c)
        s.W1b = sb("W1b", [128, 1024], BF16, stack=st_l0c)
        s.b1b = sb("b1b", [128, 1024], BF16, stack=st_l0c)
        s.W2b = sb("W2b", [128, 1024], BF16, stack=st_l0c)
        s.b2b = sb("b2b", [128, 64], BF16, stack=st_l0c)
        s.woutb = sb("woutb", [64, D], BF16, stack=st_l0c)
        s.boutb = sb("boutb", [128, D], BF16, stack=st_l0c)

        self.block = self.stack.enter_context(nc.Block())

        self.ph_0()
        if debug == "ph0":
            self.ph_w_pass1()
            self.ph_w_finish()
            dd = nc.dram_tensor("d_misc", [128, 96], F32, kind="ExternalOutput").ap()
            self.dbg_outs.append("d_misc")
            dmx = sb("dmx", [128, 96], F32)
            block = self.block
            @block.vector
            def _(vector):
                self.gate(vector)
                vector.memset(dmx[:, :], 0.0)
                vector.tensor_copy(out=dmx[:, 0:64], in_=s.cswb[:, :])
                vector.tensor_copy(out=dmx[:, 64:65], in_=s.c8v[:, :])
                vector.tensor_copy(out=dmx[:, 65:67], in_=s.wneg[:, :])
                vector.tensor_copy(out=dmx[:, 67:71], in_=s.wtau[:, :])
                vector.tensor_copy(out=dmx[0:1, 73:73 + 16], in_=s.t2[0:1, :])
                self.inc(vector.tensor_copy(out=dmx[:, 71:73], in_=s.jv[:, :]), "VE")
            self.dbg_store([(dd[:, :], dmx[:, :])])
            return self.fin()

        st_ysb = contextlib.ExitStack()
        self._open_stacks.append(st_ysb)
        s.ysb = sb("ysb", [128, NT * D], BF16, stack=st_ysb)
        st_a = contextlib.ExitStack()
        self._open_stacks.append(st_a)
        self.ph_w_alloc()
        s.ln0T = sb("ln0T", [128, DC * T], BF16, stack=st_a)
        self.ph_a()
        self.ph_w_pass1()
        self.ph_w_stats()
        if debug == "pha":
            dd = nc.dram_tensor("d_ln0T", [128, DC * T], BF16, kind="ExternalOutput").ap()
            self.dbg_outs.append("d_ln0T")
            self.dbg_store([(dd[:, :], s.ln0T[:, :])])
            return self.fin()

        self.ph_b()
        st_a.close()
        if debug == "phb":
            dd = nc.dram_tensor("d_ysb", [128, NT * D], BF16, kind="ExternalOutput").ap()
            self.dbg_outs.append("d_ysb")
            self.dbg_store([(dd[:, :], s.ysb[:, :])])
            return self.fin()

        self.ph_c1()
        self._stw2.close()
        self._stw.close()
        if debug == "phw":
            d1 = nc.dram_tensor("d_sw1", [256, 16384], BF16, kind="ExternalOutput").ap()
            d2_ = nc.dram_tensor("d_sw2", [4096, 1024], BF16, kind="ExternalOutput").ap()
            self.dbg_outs += ["d_sw1", "d_sw2"]
            self.dbg_store([(d1[:, :], s.scr_w1[:, :]), (d2_[:, :], s.scr_w2[:, :])])
            return self.fin()
        st_ysb.close()
        self._st_l0c.close()
        if debug == "phc1":
            dd = nc.dram_tensor("d_x2", [T, D], F32, kind="ExternalOutput").ap()
            self.dbg_outs.append("d_x2")
            block = self.block
            w, inc = self.w, self.inc
            @block.sync
            def _(sync):
                for nm in ("VE", "SE", "TE", "GE"):
                    w(sync, nm)
                self.dwait(sync, "SD")
            xl = sb("xl2", [128, D], F32)
            for tt in range(NT):
                @block.sync
                def _(sync, tt=tt):
                    if tt > 0:
                        self.dwait(sync, "ST")
                    self.dinc(sync.dma_start(out=xl[:, :],
                                             in_=s.scr_x2[tt * 128:(tt + 1) * 128, :]), "LD")
                    self.dwait(sync, "LD")
                    self.dinc(sync.dma_start(out=dd[tt * 128:(tt + 1) * 128, :], in_=xl[:, :]),
                              "ST")
            return self.fin()

        st_x = contextlib.ExitStack()
        self._open_stacks.append(st_x)
        s.x2win = sb("x2win", [128, ENT * D], F32, stack=st_x)
        st_de = contextlib.ExitStack()
        self._open_stacks.append(st_de)
        s.latT = sb("latT", [128, T], BF16, stack=st_de)
        s.qpT = sb("qpT", [128, 4 * ETK], BF16, stack=st_de)
        st_m = contextlib.ExitStack()
        self._open_stacks.append(st_m)
        s.Mbuf = sb("Mbuf", [128, DC * 512], BF16, stack=st_m)
        self.ph_mn(do_m=True, do_n=False)
        if debug == "phmn":
            s.Nbuf = sb("Nbuf", [128, 4 * D], BF16, stack=st_m)
            self.ph_mn(do_m=False, do_n=True)
            dm = nc.dram_tensor("d_M", [128, DC * 512], BF16, kind="ExternalOutput").ap()
            dn = nc.dram_tensor("d_N", [128, 4 * D], BF16, kind="ExternalOutput").ap()
            self.dbg_outs += ["d_M", "d_N"]
            self.dbg_store([(dm[:, :], s.Mbuf[:, :]), (dn[:, :], s.Nbuf[:, :])])
            return self.fin()

        st_c2 = contextlib.ExitStack()
        self._open_stacks.append(st_c2)
        s.ln2T = sb("ln2T", [128, DC * T], BF16, stack=st_c2)
        self.ph_c2()
        if debug == "phc2":
            d1 = nc.dram_tensor("d_ln2T", [128, DC * T], BF16, kind="ExternalOutput").ap()
            d2_ = nc.dram_tensor("d_x2win", [128, ENT * D], F32, kind="ExternalOutput").ap()
            self.dbg_outs += ["d_ln2T", "d_x2win"]
            self.dbg_store([(d1[:, :], s.ln2T[:, :]), (d2_[:, :], s.x2win[:, :])])
            return self.fin()

        st_d = contextlib.ExitStack()
        self._open_stacks.append(st_d)
        s.ln2Twin = sb("ln2Twin", [128, DC * 1152], BF16, stack=st_d)
        self.ph_d()
        st_d.close()
        st_c2.close()
        st_m.close()
        if debug == "phd":
            d1 = nc.dram_tensor("d_latT", [128, T], BF16, kind="ExternalOutput").ap()
            d2_ = nc.dram_tensor("d_qpT", [128, 4 * ETK], BF16, kind="ExternalOutput").ap()
            self.dbg_outs += ["d_latT", "d_qpT"]
            self.dbg_store([(d1[:, :], s.latT[:, :]), (d2_[:, :], s.qpT[:, :])])
            return self.fin()

        st_ef = contextlib.ExitStack()
        self._open_stacks.append(st_ef)
        s.oTn = sb("oTn", [128, 4 * ETK], BF16, stack=st_ef)
        st_n = contextlib.ExitStack()
        self._open_stacks.append(st_n)
        s.Nbuf = sb("Nbuf", [128, 4 * D], BF16, stack=st_n)
        st_e = contextlib.ExitStack()
        self._open_stacks.append(st_e)
        self.ph_e(st_e)
        st_e.close()
        if debug == "phe":
            d1 = nc.dram_tensor("d_oTn", [128, 4 * ETK], BF16, kind="ExternalOutput").ap()
            self.dbg_outs += ["d_oTn"]
            self.dbg_store([(d1[:, :], s.oTn[:, :])])
            return self.fin()

        self.ph_mn(do_m=False, do_n=True)
        st_f = contextlib.ExitStack()
        self._open_stacks.append(st_f)
        self.ph_f(st_f)
        st_f.close()
        if debug == "phf":
            d1 = nc.dram_tensor("d_x3", [128, ENT * D], F32, kind="ExternalOutput").ap()
            self.dbg_outs += ["d_x3"]
            self.dbg_store([(d1[:, :], s.x2win[:, :])])
            return self.fin()
        st_n.close()
        st_ef.close()
        st_de.close()

        if debug == "phg0":
            self.ph_g(g0_only=True)
            st_x.close()
            return self.fin()
        if debug == "phg1":
            self.ph_g()
            st_x.close()
            return self.fin()
        self.ph_g(g0_only=(debug == "phg0"))
        st_x.close()
        return self.fin()

    def ph_0(self):
        s, nc, block = self, self.nc, self.block
        cnt, w, inc, dinc = self.cnt, self.w, self.inc, self.dinc
        @block.sync
        def _(sync):
            for dst, src in ((s.idf, s.identf), (s.idb, s.identb),
                             (s.bi_s, s.bi_pp), (s.br_s, s.br_pp),
                             (s.cw_g, s.cwr), (s.cb_g, s.cbr)):
                dinc(sync.dma_start(out=dst[:, :], in_=src[:, :]), "LD")
            dinc(sync.dma_start(out=s.jv[:, :], in_=s.jsel.broadcast_to((128, 2))), "LD")
            dinc(sync.dma_start(out=s.t2[:, 0:1], in_=s.lam[:, :]), "LD")
        @block.gpsimd
        def _(g):
            dinc(g.dma_start(out=s.binb[:, :], in_=s.binr.broadcast_to((128, 64))), "GD")
            dinc(g.dma_start(out=s.W1b[:, :], in_=s.W1r.broadcast_to((128, 1024))), "GD")
            dinc(g.dma_start(out=s.b1b[:, :], in_=s.b1r.broadcast_to((128, 1024))), "GD")
            dinc(g.dma_start(out=s.W2b[:, :], in_=s.W2r.broadcast_to((128, 1024))), "GD")
            dinc(g.dma_start(out=s.b2b[:, :], in_=s.b2r.broadcast_to((128, 64))), "GD")
            dinc(g.dma_start(out=s.boutb[:, :], in_=s.boutr.broadcast_to((128, D))), "GD")
        self._gd_ph0 = self.dsnap("GD")
        ld0 = self.dsnap("LD")

        stp = contextlib.ExitStack()
        ps00 = self.psum("ps00", [128, 512], F32, stack=stp)
        wtm = self.sb("wtm", [128, 2048], F32, stack=stp)
        @block.sync
        def _(sync):
            self.dwait(sync, "LD")
            for dc in range(DC):
                dinc(sync.dma_start(out=wtm[:, dc * 64:dc * 64 + 64],
                                   in_=s.Win[dc * 128:(dc + 1) * 128, :]), "LD")
            dinc(sync.dma_start(out=wtm[0:64, 512:512 + D], in_=s.Wout[:, :]), "LD")
            for dc in range(DC):
                dinc(sync.dma_start(out=wtm[:, 1536 + dc * 64:1536 + (dc + 1) * 64],
                                   in_=s.WkvdT[dc * 128:(dc + 1) * 128, :]), "LD")
        ldw = self.dsnap("LD")
        @block.vector
        def _(vector):
            vector.memset(s.onesf1[:, :], 1.0)
            vector.memset(s.t2[:, 1:16], 0.0)
            vector.memset(s.eshv[:, :], ESH)
            vector.memset(s.sA[:, :], 0.0)
            inc(vector.memset(s.onescol[:, :], 1.0), "VE")
            self.dwait(vector, "LD", ldw)
            vector.tensor_copy(out=s.winb[:, :], in_=wtm[:, 0:512])
            vector.tensor_copy(out=s.woutb[:, :], in_=wtm[0:64, 512:512 + D])
            kv2 = s.kvdb.rearrange("p (dc two f) -> p dc two f", two=2, f=LAT)
            srcv = wtm[:, 1536:1536 + 512].rearrange("p (dc o f) -> p dc o f", o=1, f=LAT)
            vector.tensor_copy(out=kv2[:, :, 0:1, :], in_=srcv)
            inc(vector.tensor_copy(out=kv2[:, :, 1:2, :], in_=srcv), "VE")
        ve0 = cnt["VE"]
        @block.scalar
        def _(scalar):
            self.dwait(scalar, "LD", ld0)
            self.chain(scalar.activation(out=s.t2[:, 1:2], in_=s.t2[:, 0:1], func=AF.Exp),
                       scalar, "SE")
            self.chain(scalar.activation(out=s.t2[:, 2:3], in_=s.t2[:, 1:2], func=AF.Ln,
                                         bias=1.0), scalar, "SE")
            inc(scalar.mul(out=s.t2[:, 3:4], in_=s.t2[:, 2:3], mul=-8.0), "SE")
        se0 = cnt["SE"]
        @block.tensor
        def _(tensor):
            w(tensor, "VE", ve0)
            w(tensor, "SE", se0)
            tensor.matmul(ps00[:, 0:1], s.onesf1[:, :], s.t2[:, 3:4], start=True, stop=True)
            for dc in range(DC):
                i = tensor.matmul(ps00[0:1, 64:128], s.onescol[:, :],
                                  wtm[:, dc * 64:(dc + 1) * 64],
                                  start=(dc == 0), stop=(dc == DC - 1))
            inc(i, "TE")
        te0 = cnt["TE"]
        @block.scalar
        def _(scalar):
            w(scalar, "TE", te0)
            scalar.copy(out=s.c8v[:, :], in_=ps00[:, 0:1])
            inc(scalar.copy(out=wtm[0:1, 0:64], in_=ps00[0:1, 64:128]), "SE")
        se1 = cnt["SE"]
        @block.tensor
        def _(tensor):
            w(tensor, "SE", se1)
            inc(tensor.matmul(ps00[:, 64:128], s.onesf1[:, :], wtm[0:1, 0:64],
                              start=True, stop=True), "TE")
        te1 = cnt["TE"]
        @block.scalar
        def _(scalar):
            w(scalar, "TE", te1)
            inc(scalar.copy(out=s.cswb[:, :], in_=ps00[:, 64:128]), "SE")
        stp.close()

    def _w_piece_src(self, wi, c):
        s = self
        if wi == 0:
            dc, cp = c // 4, c % 4
            return s.w1T[dc * 128:(dc + 1) * 128, cp * 1024:(cp + 1) * 1024]
        return s.w2T[c * 128:(c + 1) * 128, :]

    def ph_w_alloc(self):
        self._stw = contextlib.ExitStack()
        self._open_stacks.append(self._stw)
        self._wld = self.sb("wld", [128, 2 * 1024], F32, stack=self._stw)
        self._wq_a = self.sb("wq_a", [128, 1024], F32, stack=self._stw)
        self._wq_o = self.sb("wq_o", [128, 1024], BF16, stack=self._stw)
        self._wsum = self.sb("wsum", [128, 64], F32, stack=self._stw)

    def ph_w_pass1(self):
        s, nc, block = self, self.nc, self.block
        cnt, w, inc, dinc = self.cnt, self.w, self.inc, self.dinc
        wld, wsum = self._wld, self._wsum
        self._se_base = cnt["SE"]
        snaps = []
        for k in range(64):
            wi, c = k // 32, k % 32
            @block.gpsimd
            def _(g, wi=wi, c=c, k=k):
                buf = wld[:, (k % 2) * 1024:(k % 2 + 1) * 1024]
                if k >= 2:
                    w(g, "SE", self._se_base + k - 1)
                dinc(g.dma_start(out=buf, in_=self._w_piece_src(wi, c)), "GD")
            snaps.append(self.dsnap("GD"))
        @block.scalar
        def _(scalar, snaps=tuple(snaps)):
            for k in range(64):
                self.dwait(scalar, "GD", snaps[k])
                inc(scalar.activation(out=self._wq_a[:, :],
                                      in_=wld[:, (k % 2) * 1024:(k % 2 + 1) * 1024],
                                      func=AF.Abs,
                                      accum_out=wsum[:, k:k + 1]), "SE")
        self._se_p1 = cnt["SE"]

    def ph_w_stats(self):
        s, nc, block = self, self.nc, self.block
        cnt, w, inc, dinc = self.cnt, self.w, self.inc, self.dinc
        wld, wsum, wq_a, wq_o = self._wld, self._wsum, self._wq_a, self._wq_o
        self._stw2 = stw2 = contextlib.ExitStack()
        self._open_stacks.append(stw2)
        psw = self.psum("psw", [128, 512], F32, stack=stw2)
        @block.vector
        def _(vector):
            w(vector, "SE", self._se_p1)
            vector.reduce_sum(out=s.sA[:, 8:9], in_=wsum[:, 0:32], axis=AX.X)
            inc(vector.reduce_sum(out=s.sA[:, 9:10], in_=wsum[:, 32:64], axis=AX.X), "VE")
        vev = cnt["VE"]
        @block.tensor
        def _(tensor):
            w(tensor, "VE", vev)
            tensor.transpose(psw[0:1, 0:128], s.sA[:, 8:9], s.idf[:, :])
            inc(tensor.transpose(psw[0:1, 128:256], s.sA[:, 9:10], s.idf[:, :]), "TE")
        tev = cnt["TE"]
        @block.vector
        def _(vector):
            w(vector, "TE", tev)
            self.chain(vector.reduce_sum(out=s.t2[0:1, 8:9], in_=psw[0:1, 0:128], axis=AX.X),
                       vector, "VE")
            self.chain(vector.reduce_sum(out=s.t2[0:1, 9:10], in_=psw[0:1, 128:256],
                                         axis=AX.X), vector, "VE")
            numel = float(D * HID)
            self.chain(vector.tensor_scalar(out=s.t2[0:1, 10:12], in0=s.t2[0:1, 8:10],
                                            scalar1=1.0 / numel, scalar2=None, op0=ALU.mult),
                       vector, "VE")
            inc(vector.tensor_scalar(out=s.t2[0:1, 12:14], in0=s.t2[0:1, 10:12],
                                     scalar1=0.5, scalar2=None, op0=ALU.mult), "VE")
        vev3 = cnt["VE"]
        @block.tensor
        def _(tensor):
            w(tensor, "VE", vev3)
            inc(tensor.matmul(psw[:, 0:4], s.onesf1[:, :], s.t2[0:1, 10:14],
                              start=True, stop=True), "TE")
        tev2 = cnt["TE"]
        @block.vector
        def _(vector):
            w(vector, "TE", tev2)
            vector.tensor_scalar(out=s.wneg[:, 0:2], in0=psw[:, 0:2], scalar1=-1.0,
                                 scalar2=None, op0=ALU.mult)
            vector.tensor_copy(out=s.wtau[:, 0:1], in_=psw[:, 2:3])
            vector.tensor_scalar(out=s.wtau[:, 1:2], in0=psw[:, 2:3], scalar1=-1.0,
                                 scalar2=None, op0=ALU.mult)
            vector.tensor_copy(out=s.wtau[:, 2:3], in_=psw[:, 3:4])
            inc(vector.tensor_scalar(out=s.wtau[:, 3:4], in0=psw[:, 3:4], scalar1=-1.0,
                                     scalar2=None, op0=ALU.mult), "VE")
        self._vtau = cnt["VE"]
        self._w2 = dict(sd_snaps=[], ld2=[], q_cnt=[], next_q=0)

    def _emit_w2_load(self, k):
        s, block, w, dinc = self, self.block, self.w, self.dinc
        wld = self._wld
        st = self._w2
        wi, c = k // 32, k % 32
        qwait = st["q_cnt"][k - 2] if k >= 2 else None
        @block.sync
        def _(sync, wi=wi, c=c, k=k, qwait=qwait):
            buf = wld[:, (k % 2) * 1024:(k % 2 + 1) * 1024]
            if qwait is not None:
                w(sync, "VE", qwait)
            dinc(sync.dma_start(out=buf, in_=self._w_piece_src(wi, c)), "LD")
        st["ld2"].append(self.dsnap("LD"))

    def emit_w2_chunk(self, nq):
        s, nc, block = self, self.nc, self.block
        cnt, w, inc, dinc = self.cnt, self.w, self.inc, self.dinc
        wld, wq_a, wq_o = self._wld, self._wq_a, self._wq_o
        st = self._w2
        vtau = self._vtau
        sd_snaps, ld2 = st["sd_snaps"], st["ld2"]
        if st["next_q"] == 0 and nq > 0:
            for k in range(2):
                self._emit_w2_load(k)
        for k in range(st["next_q"], min(st["next_q"] + nq, 64)):
            qo = wq_o[:, :]
            wi, c = k // 32, k % 32
            tc = wi * 2
            @block.vector
            def _(vector, wi=wi, c=c, k=k, tc=tc, qo=qo):
                buf = wld[:, (k % 2) * 1024:(k % 2 + 1) * 1024]
                self.dwait(vector, "LD", ld2[k])
                if k == 0:
                    w(vector, "VE", vtau)
                if k >= 1:
                    self.dwait(vector, "SD", sd_snaps[k - 1])
                vector.tensor_single_scalar(out=wq_a[:, :], in_=buf,
                                            scalar=s.wtau[:, tc:tc + 1], op=ALU.is_gt)
                inc(vector.scalar_tensor_tensor(out=qo, in0=buf,
                                                scalar=s.wtau[:, tc + 1:tc + 2],
                                                in1=wq_a[:, :],
                                                op0=ALU.is_lt, op1=ALU.subtract), "VE")
            st["q_cnt"].append(cnt["VE"])
            veq = cnt["VE"]
            @block.gpsimd
            def _(g, wi=wi, c=c, veq=veq, qo=qo):
                w(g, "VE", veq)
                if wi == 0:
                    dc, cp = c // 4, c % 4
                    for h8 in range(8):
                        hc = cp * 8 + h8
                        dinc(g.dma_start(
                            out=s.scr_w1[hc * DC + dc:hc * DC + dc + 1, :]
                            .rearrange("a (p f) -> (a p) f", p=128),
                            in_=qo[:, h8 * 128:(h8 + 1) * 128]), "SD")
                else:
                    dinc(g.dma_start(out=s.scr_w2[c * 128:(c + 1) * 128, :],
                                     in_=qo[:, :]), "SD")
            sd_snaps.append(self.dsnap("SD"))
            if k + 2 < 64:
                self._emit_w2_load(k + 2)
        st["next_q"] = min(st["next_q"] + nq, 64)

    def ph_a(self):
        s, nc, block = self, self.nc, self.block
        cnt, w, inc, dinc = self.cnt, self.w, self.inc, self.dinc
        sta = contextlib.ExitStack()
        xt = self.sb("xt", [128, 2 * D], F32, stack=sta)
        ln0b = self.sb("ln0b", [128, 2 * D], BF16, stack=sta)
        pst = self.psum("pst", [128, 2 * 1024], BF16, stack=sta)
        ln0T_v = s.ln0T.rearrange("p (dc f) -> p dc f", f=T)
        evac_done = []
        for tt in range(NT):
            p = tt % 2
            @block.sync
            def _(sync, tt=tt, p=p):
                if tt >= 2:
                    w(sync, "VE", evac_done[tt - 2][0])
                dinc(sync.dma_start(out=xt[:, p * D:(p + 1) * D],
                                   in_=s.xb[tt * 128:(tt + 1) * 128, :]), "LD")
            ldv = self.dsnap("LD")
            @block.vector
            def _(vector, p=p, ldv=ldv):
                self.dwait(vector, "LD", ldv)
                inc(self.stats_v(vector, xt[:, p * D:p * D + 512],
                                 xt[:, p * D + 512:(p + 1) * D], 0), "VE")
            ve1 = cnt["VE"]
            @block.scalar
            def _(scalar, ve1=ve1):
                w(scalar, "VE", ve1)
                inc(self.stats_s(scalar, 0), "SE")
            se1 = cnt["SE"]
            @block.vector
            def _(vector, p=p, se1=se1):
                w(vector, "SE", se1)
                inc(vector.tensor_scalar(out=ln0b[:, p * D:(p + 1) * D],
                                         in0=xt[:, p * D:(p + 1) * D],
                                         scalar1=s.sA[:, 0:1], scalar2=s.sA[:, 1:2],
                                         op0=ALU.subtract, op1=ALU.mult), "VE")
            ve2 = cnt["VE"]
            @block.tensor
            def _(tensor, tt=tt, p=p, ve2=ve2):
                w(tensor, "VE", ve2)
                if tt >= 2:
                    w(tensor, "VE", evac_done[tt - 2][1])
                for dc in range(DC):
                    i = tensor.transpose(pst[:, p * 1024 + dc * 128:p * 1024 + (dc + 1) * 128],
                                         ln0b[:, p * D + dc * 128:p * D + (dc + 1) * 128],
                                         s.idb[:, :])
                inc(i, "TE")
            te1 = cnt["TE"]
            @block.vector
            def _(vector, tt=tt, p=p, te1=te1):
                w(vector, "TE", te1)
                inc(vector.tensor_copy(out=ln0T_v[:, :, tt * 128:(tt + 1) * 128],
                                       in_=pst[:, p * 1024:(p + 1) * 1024]
                                       .rearrange("p (dc f) -> p dc f", f=128)), "VE")
            evac_done.append((ve2, cnt["VE"]))
        sta.close()

    def ph_b(self):
        s, nc, block = self, self.nc, self.block
        cnt, w, inc, dinc = self.cnt, self.w, self.inc, self.dinc
        stb = contextlib.ExitStack()
        wbA = self.sb("wbA", [128, 2 * 1024], BF16, stack=stb)
        wbB = self.sb("wbB", [128, 2 * 1024], BF16, stack=stb)
        itb = self.sb("itb", [128, T], BF16, stack=stb)
        atf = self.sb("atf", [128, T], F32, stack=stb)
        asqc = self.sb("asqc", [128, 2 * 512], F32, stack=stb)
        ysc = self.sb("ysc", [128, T], BF16, stack=stb)
        psI = self.psum("psI", [128, 1024], F32, stack=stb)
        psR = self.psum("psR", [128, 1024], F32, stack=stb)
        psT = self.psum("psT", [128, 2 * 1024], BF16, stack=stb)
        ysb_v = s.ysb.rearrange("p (k f) -> p k f", f=D)

        cast_done = []
        scan_done = []
        g1_done = []
        tp_done = []
        tp_te = []

        def emit_tp(dcq):
            for g8 in range(4):
                bk = g8 % 2
                @block.tensor
                def _(tensor, dcq=dcq, g8=g8, bk=bk):
                    w(tensor, "VE", scan_done[dcq])
                    if g8 >= 2 or (dcq >= 1 and len(tp_done) > 0):
                        w(tensor, "VE")
                    for k in range(8):
                        tti = g8 * 8 + k
                        i = tensor.transpose(psT[:, bk * 1024 + k * 128:bk * 1024 + (k + 1) * 128],
                                             ysc[:, tti * 128:(tti + 1) * 128],
                                             s.idb[:, :])
                    inc(i, "TE")
                te_t = cnt["TE"]
                @block.vector
                def _(vector, dcq=dcq, g8=g8, bk=bk, te_t=te_t):
                    w(vector, "TE", te_t)
                    inc(vector.tensor_copy(
                        out=ysb_v[:, g8 * 8:(g8 + 1) * 8, dcq * 128:(dcq + 1) * 128],
                        in_=psT[:, bk * 1024:(bk + 1) * 1024]
                        .rearrange("p (k f) -> p k f", f=128)), "VE")
            tp_done.append(cnt["VE"])
            tp_te.append(cnt["TE"])

        @block.sync
        def _(sync):
            self.gate(sync)
        @block.gpsimd
        def _(g):
            self.gate(g, dma=False)
        se0b = cnt["SE"]

        def sigc(dc, proj, tb):
            return se0b + dc * 25 + proj * 8 + tb + 1

        for dc in range(DC):
            p = dc % 2
            @block.gpsimd
            def _(g, dc=dc, p=p):
                if dc >= 2:
                    w(g, "SE", sigc(dc - 2, 1, 7))
                for kc in range(DC):
                    dinc(g.dma_start(out=wbA[:, p * 1024 + kc * 128:p * 1024 + (kc + 1) * 128],
                                     in_=s.WiT[kc * 128:(kc + 1) * 128, dc * 128:(dc + 1) * 128]),
                         "GD")
                    dinc(g.dma_start(out=wbB[:, p * 1024 + kc * 128:p * 1024 + (kc + 1) * 128],
                                     in_=s.WrT[kc * 128:(kc + 1) * 128, dc * 128:(dc + 1) * 128]),
                         "GD")
            cast_done.append(self.dsnap("GD"))

            te_mm_l = {}
            for proj in range(2):
                ps_ = psI if proj == 0 else psR
                wb_ = wbA if proj == 0 else wbB
                for tb in range(DC):
                    bk = tb % 2
                    @block.tensor
                    def _(tensor, dc=dc, p=p, proj=proj, tb=tb, bk=bk, ps_=ps_, wb_=wb_):
                        if tb == 0 and proj == 0:
                            self.dwait(tensor, "GD", cast_done[dc])
                        if tb >= 2:
                            w(tensor, "SE", sigc(dc, proj, tb - 2))
                        elif dc > 0:
                            w(tensor, "SE", sigc(dc - 1, proj, tb + 6))
                        for kc in range(DC):
                            i = tensor.matmul(
                                ps_[:, bk * 512:(bk + 1) * 512],
                                wb_[:, p * 1024 + kc * 128:p * 1024 + (kc + 1) * 128],
                                s.ln0T[:, kc * T + tb * 512:kc * T + (tb + 1) * 512],
                                start=(kc == 0), stop=(kc == DC - 1))
                        inc(i, "TE")
                    te_mm_l[(proj, tb)] = cnt["TE"]
            @block.vector
            def _(vector, dc=dc):
                w(vector, "SE", se0b + dc * 25 + 8)
                inc(vector.tensor_tensor(out=itb[:, :], in0=itb[:, :],
                                         in1=s.ln0T[:, dc * T:(dc + 1) * T],
                                         op=ALU.mult), "VE")
            ve_sq_l = [None] * 8

            def emit_sq(ch, dc=dc):
                @block.vector
                def _(vector, ch=ch, dc=dc):
                    if ch == 0:
                        w(vector, "SE", se0b + dc * 25 + 17)
                    p2 = ch % 2
                    inc(vector.tensor_tensor(out=asqc[:, p2 * 512:(p2 + 1) * 512],
                                             in0=atf[:, ch * 512:(ch + 1) * 512],
                                             in1=atf[:, ch * 512:(ch + 1) * 512],
                                             op=ALU.mult), "VE")
                ve_sq_l[ch] = cnt["VE"]

            def emit_smul(ch, dc=dc):
                @block.vector
                def _(vector, ch=ch, dc=dc):
                    w(vector, "SE", se0b + dc * 25 + 18 + ch)
                    p2 = ch % 2
                    inc(vector.tensor_tensor(out=itb[:, ch * 512:(ch + 1) * 512],
                                             in0=itb[:, ch * 512:(ch + 1) * 512],
                                             in1=asqc[:, p2 * 512:(p2 + 1) * 512],
                                             op=ALU.mult), "VE")

            emit_sq(0)
            emit_sq(1)
            for ch in range(8):
                emit_smul(ch)
                if ch + 2 < 8:
                    emit_sq(ch + 2)
            ve_g = cnt["VE"]
            @block.scalar
            def _(scalar, dc=dc, te_mm_l=dict(te_mm_l), ve_sq_l=tuple(ve_sq_l)):
                for proj in range(2):
                    ps_ = psI if proj == 0 else psR
                    bias_ = s.bi_s if proj == 0 else s.br_s
                    dst = itb if proj == 0 else atf
                    for tb in range(DC):
                        bk = tb % 2
                        w(scalar, "TE", te_mm_l[(proj, tb)])
                        if dc >= 1 and tb == 0 and proj == 0:
                            w(scalar, "VE", g1_done[dc - 1])
                        inc(scalar.activation(out=dst[:, tb * 512:(tb + 1) * 512],
                                              in_=ps_[:, bk * 512:(bk + 1) * 512],
                                              func=AF.Sigmoid, bias=bias_[:, dc:dc + 1]),
                            "SE")
                inc(scalar.activation(out=atf[:, :], in_=atf[:, :], func=AF.Exp,
                                      scale=s.c8v[:, 0:1]), "SE")
                for ch in range(8):
                    p2 = ch % 2
                    w(scalar, "VE", ve_sq_l[ch])
                    inc(scalar.activation(out=asqc[:, p2 * 512:(p2 + 1) * 512],
                                          in_=asqc[:, p2 * 512:(p2 + 1) * 512],
                                          func=AF.Sqrt, scale=-1.0, bias=1.0), "SE")
            assert cnt["SE"] == se0b + dc * 25 + 25
            if dc >= 1:
                emit_tp(dc - 1)
            @block.vector
            def _(vector, dc=dc, ve_g=ve_g):
                if dc >= 1:
                    w(vector, "TE", tp_te[dc - 1])
                inc(vector.tensor_tensor_scan(out=ysc[:, :], data0=atf[:, :],
                                              data1=itb[:, :], initial=0.0,
                                              op0=ALU.mult, op1=ALU.add), "VE")
            scan_done.append(cnt["VE"])
            g1_done.append(cnt["VE"])
        emit_tp(DC - 1)
        stb.close()

    def ph_c1(self):
        s, nc, block = self, self.nc, self.block
        cnt, w, inc, dinc = self.cnt, self.w, self.inc, self.dinc
        stc = contextlib.ExitStack()
        xt = self.sb("xtc", [128, 2 * D], F32, stack=stc)
        ysf = self.sb("ysfc", [128, D], F32, stack=stc)
        x1t = self.sb("x1t", [128, D], F32, stack=stc)
        x2t = self.sb("x2t", [128, 2 * D], F32, stack=stc)
        dl = self.sb("dl", [128, D], F32, stack=stc)
        x1bb = self.sb("x1bb", [128, D], BF16, stack=stc)
        x1Tt = self.sb("x1Tt", [128, D], BF16, stack=stc)
        hhf = self.sb("hhf", [128, 1024], F32, stack=stc)
        hh3 = self.sb("hh3", [128, 1024], BF16, stack=stc)
        sff = self.sb("sff", [128, 256], F32, stack=stc)
        g1b = self.sb("g1b", [128, 64], BF16, stack=stc)
        gTt = self.sb("gTt", [64, 128], BF16, stack=stc)
        psB = self.psum("psB", [128, 1024], BF16, stack=stc)
        psS = self.psum("psS", [128, 512], F32, stack=stc)
        psG = self.psum("psG", [64, 128], BF16, stack=stc)
        psO = self.psum("psO", [128, 1024], F32, stack=stc)

        st_done = []
        @block.sync
        def _(sync):
            self.gate(sync)
        for tt in range(NT):
            self.emit_w2_chunk(2)
            p = tt % 2
            @block.sync
            def _(sync, tt=tt, p=p):
                if tt >= 2:
                    w(sync, "VE", st_done[tt - 2][0])
                dinc(sync.dma_start(out=xt[:, p * D:(p + 1) * D],
                                   in_=s.xb[tt * 128:(tt + 1) * 128, :]), "LD")
            ldv = self.dsnap("LD")
            @block.vector
            def _(vector, tt=tt):
                if tt >= 2:
                    self.dwait(vector, "SD", st_done[tt - 2][1])
                vector.tensor_copy(out=ysf[:, :], in_=s.ysb[:, tt * D:(tt + 1) * D])
                inc(self.stats_v(vector, ysf[:, 0:512], ysf[:, 512:1024], 0), "VE")
            ve1 = cnt["VE"]
            @block.scalar
            def _(scalar, ve1=ve1):
                w(scalar, "VE", ve1)
                inc(self.stats_s(scalar, 0), "SE")
            se1 = cnt["SE"]
            @block.vector
            def _(vector, p=p, ldv=ldv, se1=se1):
                w(vector, "SE", se1)
                self.dwait(vector, "LD", ldv)
                vector.tensor_scalar(out=dl[:, :], in0=ysf[:, :], scalar1=s.sA[:, 0:1],
                                     scalar2=s.sA[:, 1:2], op0=ALU.subtract, op1=ALU.mult)
                vector.tensor_tensor(out=x1t[:, :], in0=xt[:, p * D:(p + 1) * D],
                                     in1=dl[:, :], op=ALU.add)
                inc(self.stats_v(vector, x1t[:, 0:512], x1t[:, 512:1024], 2), "VE")
            ve2 = cnt["VE"]
            @block.scalar
            def _(scalar, ve2=ve2):
                w(scalar, "VE", ve2)
                inc(self.stats_s(scalar, 2), "SE")
            se2 = cnt["SE"]
            @block.vector
            def _(vector, se2=se2):
                w(vector, "SE", se2)
                self.chain(vector.tensor_scalar(out=s.sA[:, 4:5], in0=s.sA[:, 2:3],
                                                scalar1=s.sA[:, 3:4], scalar2=-1.0,
                                                op0=ALU.mult, op1=ALU.mult), vector, "VE")
                inc(vector.tensor_copy(out=x1bb[:, :], in_=x1t[:, :]), "VE")
            ve3 = cnt["VE"]
            @block.tensor
            def _(tensor, tt=tt, ve3=ve3):
                w(tensor, "VE", ve3)
                if tt >= 1:
                    w(tensor, "VE")
                for dc in range(DC):
                    i = tensor.transpose(psB[:, dc * 128:(dc + 1) * 128],
                                         x1bb[:, dc * 128:(dc + 1) * 128], s.idb[:, :])
                inc(i, "TE")
            te1 = cnt["TE"]
            @block.vector
            def _(vector, te1=te1):
                w(vector, "TE", te1)
                inc(vector.tensor_copy(out=x1Tt[:, :], in_=psB[:, :]), "VE")
            ve4 = cnt["VE"]
            @block.tensor
            def _(tensor, tt=tt, ve4=ve4):
                w(tensor, "VE", ve4)
                for dc in range(DC):
                    i = tensor.matmul(psS[:, 0:64], x1Tt[:, dc * 128:(dc + 1) * 128],
                                      s.winb[:, dc * 64:(dc + 1) * 64],
                                      start=(dc == 0), stop=(dc == DC - 1))
                inc(i, "TE")
            te2 = cnt["TE"]
            @block.vector
            def _(vector, te2=te2):
                w(vector, "TE", te2)
                self.chain(vector.tensor_scalar(out=sff[:, 0:64], in0=psS[:, 0:64],
                                                scalar1=s.sA[:, 3:4], scalar2=None,
                                                op0=ALU.mult), vector, "VE")
                self.chain(vector.scalar_tensor_tensor(out=sff[:, 64:128], in0=s.cswb[:, :],
                                                       scalar=s.sA[:, 4:5], in1=sff[:, 0:64],
                                                       op0=ALU.mult, op1=ALU.add), vector, "VE")
                inc(vector.tensor_tensor(out=sff[:, 128:192], in0=sff[:, 64:128],
                                         in1=s.binb[:, :], op=ALU.add), "VE")
            ve5 = cnt["VE"]
            @block.gpsimd
            def _(g, ve5=ve5, tt=tt):
                if tt == 0:
                    self.dwait(g, "GD", self._gd_ph0)
                w(g, "VE", ve5)
                s_b = sff[:, 128:192].rearrange("p (m o) -> p m o", o=1).broadcast_to((128, 64, 16))
                g.tensor_tensor(out=hhf[:, :].rearrange("p (m i) -> p m i", i=16),
                                in0=s_b, in1=s.W1b[:, :].rearrange("p (m i) -> p m i", i=16),
                                op=ALU.mult)
                inc(g.tensor_tensor(out=hhf[:, :], in0=hhf[:, :], in1=s.b1b[:, :],
                                    op=ALU.add), "GE")
            ge1 = cnt["GE"]
            @block.scalar
            def _(scalar, ge1=ge1):
                w(scalar, "GE", ge1)
                inc(scalar.activation(out=hh3[:, :], in_=hhf[:, :], func=AF.Relu), "SE")
            se3 = cnt["SE"]
            @block.gpsimd
            def _(g, se3=se3):
                w(g, "SE", se3)
                inc(g.tensor_tensor(out=hhf[:, :], in0=hh3[:, :], in1=s.W2b[:, :],
                                    op=ALU.mult), "GE")
            ge2 = cnt["GE"]
            @block.vector
            def _(vector, ge2=ge2):
                w(vector, "GE", ge2)
                self.chain(vector.reduce_sum(out=sff[:, 192:256].rearrange("p (m o) -> p m o", o=1),
                                             in_=hhf[:, :].rearrange("p (m i) -> p m i", i=16),
                                             axis=AX.X), vector, "VE")
                inc(vector.tensor_tensor(out=g1b[:, :], in0=sff[:, 192:256], in1=s.b2b[:, :],
                                         op=ALU.add), "VE")
            ve6 = cnt["VE"]
            @block.tensor
            def _(tensor, ve6=ve6, tt=tt):
                w(tensor, "VE", ve6)
                inc(tensor.transpose(psG[0:64, :], g1b[:, :], s.idb[:, :]), "TE")
            te3 = cnt["TE"]
            @block.vector
            def _(vector, te3=te3):
                w(vector, "TE", te3)
                inc(vector.tensor_copy(out=gTt[:, :], in_=psG[0:64, :]), "VE")
            ve7 = cnt["VE"]
            @block.tensor
            def _(tensor, ve7=ve7, tt=tt):
                w(tensor, "VE", ve7)
                if tt >= 1:
                    w(tensor, "VE")
                tensor.matmul(psO[:, 0:512], gTt[:, :], s.woutb[:, 0:512], start=True, stop=True)
                inc(tensor.matmul(psO[:, 512:1024], gTt[:, :], s.woutb[:, 512:1024],
                                  start=True, stop=True), "TE")
            te4 = cnt["TE"]
            @block.vector
            def _(vector, te4=te4):
                w(vector, "TE", te4)
                vector.tensor_tensor(out=hhf[:, :], in0=psO[:, :], in1=s.boutb[:, :], op=ALU.add)
                inc(self.stats_v(vector, hhf[:, 0:512], hhf[:, 512:1024], 6), "VE")
            ve8 = cnt["VE"]
            @block.scalar
            def _(scalar, ve8=ve8):
                w(scalar, "VE", ve8)
                inc(self.stats_s(scalar, 6), "SE")
            se4 = cnt["SE"]
            @block.vector
            def _(vector, p=p, se4=se4):
                w(vector, "SE", se4)
                vector.tensor_scalar(out=dl[:, :], in0=hhf[:, :], scalar1=s.sA[:, 6:7],
                                     scalar2=s.sA[:, 7:8], op0=ALU.subtract, op1=ALU.mult)
                inc(vector.tensor_tensor(out=x2t[:, p * D:(p + 1) * D], in0=x1t[:, :],
                                         in1=dl[:, :], op=ALU.add), "VE")
            ve9 = cnt["VE"]
            @block.gpsimd
            def _(g, tt=tt, p=p, ve9=ve9):
                w(g, "VE", ve9)
                dinc(g.dma_start(out=s.scr_x2[tt * 128:(tt + 1) * 128, :],
                                in_=x2t[:, p * D:(p + 1) * D]), "SD")
            st_done.append((ve9, self.dsnap("SD")))
        stc.close()

    def ph_mn(self, do_m=True, do_n=True):
        s, nc, block = self, self.nc, self.block
        cnt, w, inc, dinc = self.cnt, self.w, self.inc, self.dinc
        stm = contextlib.ExitStack()
        sfx = "m" if do_m else "n"
        CW = D + LAT + LAT + D
        qld = self.sb("qld" + sfx, [128, 2 * CW], F32, stack=stm)
        qldb = self.sb("qldb" + sfx, [128, 2 * CW], BF16, stack=stm)
        psM = self.psum("psM" + sfx, [128, 512], F32, stack=stm)
        psN = self.psum("psN" + sfx, [128, 1024], F32, stack=stm)
        Mb_v = s.Mbuf.rearrange("p (dc f) -> p dc f", f=512) if do_m else None
        @block.sync
        def _(sync):
            self.gate(sync)
        for h in range(H):
            @block.sync
            def _(sync, h=h):
                if h >= 1:
                    w(sync, "VE")
                for oc2 in range(2):
                    oc = h * 2 + oc2
                    b0 = oc2 * CW
                    dinc(sync.dma_start(out=qld[:, b0:b0 + D],
                                       in_=s.Wq[oc * 128:(oc + 1) * 128, :]), "LD")
                    dinc(sync.dma_start(out=qld[:, b0 + D:b0 + D + LAT],
                                       in_=s.Wk_up[oc * 128:(oc + 1) * 128, :]), "LD")
                    dinc(sync.dma_start(out=qld[:, b0 + D + LAT:b0 + D + 2 * LAT],
                                       in_=s.Wv_up[oc * 128:(oc + 1) * 128, :]), "LD")
                    dinc(sync.dma_start(out=qld[:, b0 + D + 2 * LAT:b0 + CW],
                                       in_=s.WoT[oc * 128:(oc + 1) * 128, :]), "LD")
            ldm = self.dsnap("LD")
            @block.vector
            def _(vector, ldm=ldm):
                self.dwait(vector, "LD", ldm)
                inc(vector.tensor_copy(out=qldb[:, :], in_=qld[:, :]), "VE")
            vem = cnt["VE"]
            @block.tensor
            def _(tensor, h=h, vem=vem):
                w(tensor, "VE", vem)
                if h >= 1:
                    w(tensor, "VE")
                if do_m:
                    for dc in range(DC):
                        for oc2 in range(2):
                            b0 = oc2 * CW
                            i = tensor.matmul(psM[:, dc * 64:(dc + 1) * 64],
                                              qldb[:, b0 + dc * 128:b0 + (dc + 1) * 128],
                                              qldb[:, b0 + D:b0 + D + LAT],
                                              start=(oc2 == 0), stop=(oc2 == 1))
                if do_n:
                    for nb2 in range(2):
                        for oc2 in range(2):
                            b0 = oc2 * CW
                            i = tensor.matmul(psN[0:64, nb2 * 512:(nb2 + 1) * 512],
                                              qldb[:, b0 + D + LAT:b0 + D + 2 * LAT],
                                              qldb[:, b0 + D + 2 * LAT + nb2 * 512:
                                                   b0 + D + 2 * LAT + (nb2 + 1) * 512],
                                              start=(oc2 == 0), stop=(oc2 == 1))
                inc(i, "TE")
            tem = cnt["TE"]
            @block.vector
            def _(vector, h=h, tem=tem):
                w(vector, "TE", tem)
                if do_m:
                    i = vector.tensor_copy(out=Mb_v[:, :, h * 64:(h + 1) * 64],
                                           in_=psM[:, :].rearrange("p (dc f) -> p dc f", f=64))
                if do_n:
                    c = h // 2
                    r = 64 * (h % 2)
                    i = vector.tensor_copy(out=s.Nbuf[r:r + 64, c * D:(c + 1) * D],
                                           in_=psN[0:64, :])
                inc(i, "VE")
        stm.close()

    def ph_c2(self):
        s, nc, block = self, self.nc, self.block
        cnt, w, inc, dinc = self.cnt, self.w, self.inc, self.dinc
        stc = contextlib.ExitStack()
        x2l = self.sb("x2l", [128, 2 * D], F32, stack=stc)
        ln2b = self.sb("ln2b", [128, 2 * D], BF16, stack=stc)
        pst = self.psum("pst2", [128, 2 * 1024], BF16, stack=stc)
        ln2T_v = s.ln2T.rearrange("p (dc f) -> p dc f", f=T)
        done = []
        for tt in range(NT):
            p = tt % 2
            @block.sync
            def _(sync, tt=tt, p=p):
                if tt >= 2:
                    w(sync, "VE", done[tt - 2][0])
                dinc(sync.dma_start(out=x2l[:, p * D:(p + 1) * D],
                                   in_=s.scr_x2[tt * 128:(tt + 1) * 128, :]), "LD")
            ldv = self.dsnap("LD")
            @block.vector
            def _(vector, p=p, ldv=ldv):
                self.dwait(vector, "LD", ldv)
                inc(self.stats_v(vector, x2l[:, p * D:p * D + 512],
                                 x2l[:, p * D + 512:(p + 1) * D], 0), "VE")
            ve1 = cnt["VE"]
            @block.scalar
            def _(scalar, ve1=ve1):
                w(scalar, "VE", ve1)
                inc(self.stats_s(scalar, 0), "SE")
            se1 = cnt["SE"]
            @block.vector
            def _(vector, tt=tt, p=p, se1=se1):
                w(vector, "SE", se1)
                i = vector.tensor_scalar(out=ln2b[:, p * D:(p + 1) * D],
                                         in0=x2l[:, p * D:(p + 1) * D],
                                         scalar1=s.sA[:, 0:1], scalar2=s.sA[:, 1:2],
                                         op0=ALU.subtract, op1=ALU.mult)
                if tt <= ENT - 1:
                    i = vector.tensor_single_scalar(out=s.x2win[:, tt * D:(tt + 1) * D],
                                                    in_=x2l[:, p * D:(p + 1) * D],
                                                    scalar=s.jv[:, 1:2], op=ALU.mult)
                if tt >= NT - ENT:
                    v = tt - (NT - ENT)
                    i = vector.scalar_tensor_tensor(out=s.x2win[:, v * D:(v + 1) * D],
                                                    in0=x2l[:, p * D:(p + 1) * D],
                                                    scalar=s.jv[:, 0:1],
                                                    in1=s.x2win[:, v * D:(v + 1) * D],
                                                    op0=ALU.mult, op1=ALU.add)
                inc(i, "VE")
            ve2 = cnt["VE"]
            @block.tensor
            def _(tensor, tt=tt, p=p, ve2=ve2):
                w(tensor, "VE", ve2)
                if tt >= 2:
                    w(tensor, "VE", done[tt - 2][1])
                for dc in range(DC):
                    i = tensor.transpose(pst[:, p * 1024 + dc * 128:p * 1024 + (dc + 1) * 128],
                                         ln2b[:, p * D + dc * 128:p * D + (dc + 1) * 128],
                                         s.idb[:, :])
                inc(i, "TE")
            te1 = cnt["TE"]
            @block.vector
            def _(vector, tt=tt, p=p, te1=te1):
                w(vector, "TE", te1)
                inc(vector.tensor_copy(out=ln2T_v[:, :, tt * 128:(tt + 1) * 128],
                                       in_=pst[:, p * 1024:(p + 1) * 1024]
                                       .rearrange("p (dc f) -> p dc f", f=128)), "VE")
            done.append((ve2, cnt["VE"]))
        stc.close()

    def ph_d(self):
        s, nc, block = self, self.nc, self.block
        cnt, w, inc, dinc = self.cnt, self.w, self.inc, self.dinc
        std = contextlib.ExitStack()
        psL = self.psum("psL", [128, 1024], F32, stack=std)
        psQ = self.psum("psQ", [128, 1024], F32, stack=std)
        ve_ln2T = cnt["VE"]
        @block.tensor
        def _(tensor):
            self.gate(tensor, dma=False)
        for tb in range(DC):
            bk = tb % 2
            @block.tensor
            def _(tensor, tb=tb, bk=bk):
                w(tensor, "VE", ve_ln2T)
                if tb >= 2:
                    w(tensor, "VE")
                for kc in range(DC):
                    i = tensor.matmul(psL[:, bk * 512:(bk + 1) * 512],
                                      s.kvdb[:, kc * 128:(kc + 1) * 128],
                                      s.ln2T[:, kc * T + tb * 512:kc * T + (tb + 1) * 512],
                                      start=(kc == 0), stop=(kc == DC - 1))
                inc(i, "TE")
            teL = cnt["TE"]
            @block.vector
            def _(vector, tb=tb, bk=bk, teL=teL):
                w(vector, "TE", teL)
                inc(vector.tensor_copy(out=s.latT[:, tb * 512:(tb + 1) * 512],
                                       in_=psL[:, bk * 512:(bk + 1) * 512]), "VE")
        halves = [(0, 1024, [(0, 512), (512, 512)]),
                  (1024, 1152, [(0, 512), (512, 512), (1024, 128)])]
        for hw0, hwn, qcs in halves:
            @block.vector
            def _(vector, hw0=hw0, hwn=hwn):
                w(vector, "VE", ve_ln2T)
                w(vector, "TE")
                for dcc in range(DC):
                    vector.tensor_single_scalar(out=s.ln2Twin[:, dcc * 1152:dcc * 1152 + hwn],
                                                in_=s.ln2T[:, dcc * T + hw0:dcc * T + hw0 + hwn],
                                                scalar=s.jv[:, 1:2], op=ALU.mult)
                    i = vector.scalar_tensor_tensor(
                        out=s.ln2Twin[:, dcc * 1152:dcc * 1152 + hwn],
                        in0=s.ln2T[:, dcc * T + 1920 + hw0:dcc * T + 1920 + hw0 + hwn],
                        scalar=s.jv[:, 0:1],
                        in1=s.ln2Twin[:, dcc * 1152:dcc * 1152 + hwn],
                        op0=ALU.mult, op1=ALU.add)
                inc(i, "VE")
            ve_win = cnt["VE"]
            for h in range(H):
                for qi, (q0, qn) in enumerate(qcs):
                    bk = (h * len(qcs) + qi) % 2
                    @block.tensor
                    def _(tensor, h=h, qi=qi, q0=q0, qn=qn, bk=bk, hw0=hw0, ve_win=ve_win):
                        if h == 0 and qi == 0:
                            w(tensor, "VE", ve_win)
                        if h * len(qcs) + qi >= 2:
                            w(tensor, "VE")
                        r = 64 * (h % 2)
                        for dc in range(DC):
                            i = tensor.matmul(psQ[r:r + 64, bk * 512:bk * 512 + qn],
                                              s.Mbuf[:, dc * 512 + h * 64:dc * 512 + (h + 1) * 64],
                                              s.ln2Twin[:, dc * 1152 + q0:dc * 1152 + q0 + qn],
                                              start=(dc == 0), stop=(dc == DC - 1),
                                              tile_position=(0, r))
                        inc(i, "TE")
                    teQ = cnt["TE"]
                    @block.vector
                    def _(vector, h=h, q0=q0, qn=qn, bk=bk, teQ=teQ, hw0=hw0):
                        w(vector, "TE", teQ)
                        r = 64 * (h % 2)
                        qg = hw0 + q0
                        inc(vector.tensor_copy(
                            out=s.qpT[r:r + 64, (h // 2) * ETK + qg:(h // 2) * ETK + qg + qn],
                            in_=psQ[r:r + 64, bk * 512:bk * 512 + qn]), "VE")
        std.close()

    def ph_e(self, st_e):
        s, nc, block = self, self.nc, self.block
        cnt, w, inc, dinc = self.cnt, self.w, self.inc, self.dinc
        latkm = self.sb("latkm", [128, NT * 65], BF16, stack=st_e)
        ETb = self.sb("ETb", [128, 2 * 1536], BF16, stack=st_e)
        oTh = self.sb("oTh", [128, 512], F32, stack=st_e)
        ones2d = self.sb("ones2d", [128, 64], F32, stack=st_e)
        drow = self.sb("drow", [1, 512], F32, stack=st_e)
        se_prior = cnt["SE"]
        @block.vector
        def _(vector):
            self.gate(vector, dma=False)
            vector.memset(ones2d[:, :], 1.0)
            inc(vector.memset(latkm[:, :], 1.0), "VE")
        veo = cnt["VE"]
        with nc.psum_tensor("psTkm", [128, 1024], BF16) as psTkm:
            for kb in range(NT):
                bk = kb % 2
                @block.tensor
                def _(tensor, kb=kb, bk=bk, veo=veo):
                    if kb == 0:
                        w(tensor, "VE", veo)
                    if kb >= 2:
                        w(tensor, "SE")
                    inc(tensor.transpose(psTkm[:, bk * 512:bk * 512 + 64],
                                         s.latT[0:64, kb * 128:(kb + 1) * 128],
                                         s.idb[0:64, 0:64]), "TE")
                tek = cnt["TE"]
                @block.scalar
                def _(scalar, kb=kb, bk=bk, tek=tek):
                    w(scalar, "TE", tek)
                    inc(scalar.copy(out=latkm[:, kb * 65:kb * 65 + 64],
                                    in_=psTkm[:, bk * 512:bk * 512 + 64]), "SE")
        se_lat = cnt["SE"]
        psS1 = self.psum("psS1", [128, 1536], F32, stack=st_e)
        psS2 = self.psum("psS2", [128, 1536], F32, stack=st_e)
        psOv = self.psum("psOv", [128, 512], F32, stack=st_e)
        psDb = self.psum("psDb", [128, 512], F32, stack=st_e)

        groups = []
        k0 = 0
        while k0 < NT:
            gn = 2 if NT - k0 == 4 else min(3, NT - k0)
            groups.append((k0, gn))
            k0 += gn

        rDb = self.sb("rDb", [64, 512], F32, stack=st_e)
        first = True
        prev_tail = None
        NG = len(groups)

        def emit_tail_b(pt):
            ph, pq0, pqn, pteAV, pveC = pt
            @block.tensor
            def _(tensor, pqn=pqn, pveC=pveC):
                w(tensor, "VE", pveC)
                inc(tensor.matmul(psDb[0:64, 0:pqn], ones2d[64:65, 0:64],
                                  oTh[64:65, 0:pqn], start=True, stop=True), "TE")
            teD = cnt["TE"]
            @block.vector
            def _(vector, ph=ph, pq0=pq0, pqn=pqn, teD=teD):
                w(vector, "TE", teD)
                self.chain(vector.reciprocal(out=rDb[:, 0:pqn], in_=psDb[0:64, 0:pqn]),
                           vector, "VE")
                c = ph // 2
                r = 64 * (ph % 2)
                inc(vector.tensor_tensor(
                    out=s.oTn[r:r + 64, c * ETK + pq0:c * ETK + pq0 + pqn],
                    in0=oTh[0:64, 0:pqn], in1=rDb[:, 0:pqn], op=ALU.mult), "VE")

        for h in range(H):
            for qi, (q0, qn) in enumerate(QCH):
                sE0 = cnt["SE"]
                teS_l = [None] * NG

                def emit_scores(gi, h=h, q0=q0, qn=qn, first=first, sE0=sE0):
                    k0, gn = groups[gi]
                    psS = psS1 if gi % 2 == 0 else psS2
                    @block.tensor
                    def _(tensor, gi=gi, k0=k0, gn=gn, psS=psS):
                        if first and gi <= 1:
                            w(tensor, "SE", se_lat)
                        elif gi <= 1:
                            w(tensor, "SE", sE0)
                        else:
                            w(tensor, "SE", sE0 + gi - 1)
                        r = 64 * (h % 2)
                        for k in range(gn):
                            kb = k0 + k
                            i = tensor.matmul(psS[:, k * 512:k * 512 + qn],
                                              s.latT[r:r + 64, kb * 128:(kb + 1) * 128],
                                              s.qpT[r:r + 64, (h // 2) * ETK + q0:(h // 2) * ETK + q0 + qn],
                                              start=True, stop=True,
                                              tile_position=(r, 0))
                        inc(i, "TE")
                    teS_l[gi] = cnt["TE"]

                def emit_av(gi, qn=qn, sE0=sE0, pt=prev_tail):
                    k0, gn = groups[gi]
                    ping = gi % 2
                    @block.tensor
                    def _(tensor, gi=gi, k0=k0, gn=gn, ping=ping):
                        w(tensor, "SE", sE0 + gi + 1)
                        if gi == 0 and pt is not None:
                            w(tensor, "VE", pt[4])
                        for k in range(gn):
                            kb = k0 + k
                            i = tensor.matmul(
                                psOv[0:65, 0:qn],
                                latkm[:, kb * 65:(kb + 1) * 65],
                                ETb[:, ping * 1536 + k * 512:ping * 1536 + k * 512 + qn],
                                start=(kb == 0), stop=(kb == NT - 1))
                        inc(i, "TE")

                emit_scores(0)
                emit_scores(1)
                emit_av(0)
                if prev_tail is not None:
                    emit_tail_b(prev_tail)
                    prev_tail = None
                for gi in range(2, NG):
                    emit_scores(gi)
                    emit_av(gi - 1)
                emit_av(NG - 1)
                teAV = cnt["TE"]
                @block.scalar
                def _(scalar, qn=qn, teS_l=tuple(teS_l)):
                    for gi in range(NG):
                        k0, gn = groups[gi]
                        ping = gi % 2
                        psS = psS1 if ping == 0 else psS2
                        w(scalar, "TE", teS_l[gi])
                        inc(scalar.activation(
                            out=ETb[:, ping * 1536:ping * 1536 + gn * 512]
                            .rearrange("p (k f) -> p k f", f=512)[:, :, 0:qn],
                            in_=psS[:, 0:gn * 512]
                            .rearrange("p (k f) -> p k f", f=512)[:, :, 0:qn],
                            func=AF.Exp, scale=SCALE, bias=s.eshv[:, 0:1]), "SE")
                assert cnt["SE"] == sE0 + NG
                @block.vector
                def _(vector, qn=qn, teAV=teAV):
                    w(vector, "TE", teAV)
                    inc(vector.tensor_copy(out=oTh[0:65, 0:qn], in_=psOv[0:65, 0:qn]),
                        "VE")
                prev_tail = (h, q0, qn, teAV, cnt["VE"])
                first = False
        emit_tail_b(prev_tail)

    def ph_f(self, st_f):
        s, nc, block = self, self.nc, self.block
        cnt, w, inc, dinc = self.cnt, self.w, self.inc, self.dinc
        dl2 = self.sb("dl2", [128, D], F32, stack=st_f)
        psA = self.psum("psA", [128, 1024], F32, stack=st_f)
        psA2 = self.psum("psA2", [128, 1024], F32, stack=st_f)
        ve_oTn = cnt["VE"]
        @block.tensor
        def _(tensor):
            self.gate(tensor, dma=False)
        for v in range(ENT):
            pp = v % 2
            psX = psA if pp == 0 else psA2
            @block.tensor
            def _(tensor, v=v, psX=psX):
                if v == 0:
                    w(tensor, "VE", ve_oTn)
                if v >= 2:
                    w(tensor, "VE")
                for c in range(4):
                    for hf2 in range(2):
                        i = tensor.matmul(psX[:, hf2 * 512:(hf2 + 1) * 512],
                                          s.oTn[:, c * ETK + v * 128:c * ETK + (v + 1) * 128],
                                          s.Nbuf[:, c * D + hf2 * 512:c * D + (hf2 + 1) * 512],
                                          start=(c == 0), stop=(c == 3))
                inc(i, "TE")
            teA = cnt["TE"]
            @block.vector
            def _(vector, psX=psX, teA=teA):
                w(vector, "TE", teA)
                inc(self.stats_v(vector, psX[:, 0:512], psX[:, 512:1024], 0), "VE")
            ve1 = cnt["VE"]
            @block.scalar
            def _(scalar, ve1=ve1):
                w(scalar, "VE", ve1)
                inc(self.stats_s(scalar, 0), "SE")
            se1 = cnt["SE"]
            @block.vector
            def _(vector, v=v, psX=psX, se1=se1):
                w(vector, "SE", se1)
                vector.tensor_scalar(out=dl2[:, :], in0=psX[:, :], scalar1=s.sA[:, 0:1],
                                     scalar2=s.sA[:, 1:2], op0=ALU.subtract, op1=ALU.mult)
                inc(vector.tensor_tensor(out=s.x2win[:, v * D:(v + 1) * D],
                                         in0=s.x2win[:, v * D:(v + 1) * D], in1=dl2[:, :],
                                         op=ALU.add), "VE")

    def ph_g0(self, stg):
        s, nc, block = self, self.nc, self.block
        cnt, w, inc, dinc = self.cnt, self.w, self.inc, self.dinc
        st0 = contextlib.ExitStack()
        qib = self.sb("qib", [128, D], BF16, stack=st0)
        un = self.sb("un", [128, D], F32, stack=st0)
        ni = self.sb("ni", [128, D], I32, stack=st0)
        nf = self.sb("nf", [128, D], F32, stack=st0)
        gf = self.sb("gf", [128, D], F32, stack=st0)
        sxrow = self.sb("sxrow", [1, ETK], F32, stack=st0)
        psQt = self.psum("psQt", [128, 2 * 1024], BF16, stack=st0)
        psRw = self.psum("psRw", [128, 512], F32, stack=st0)
        qT_v = s.qT.rearrange("p (dc f) -> p dc f", f=ETK)
        @block.vector
        def _(vector):
            self.gate(vector)
        for v in range(ENT):
            pp = v % 2
            @block.vector
            def _(vector, v=v):
                inc(self.stats_v(vector, s.x2win[:, v * D:v * D + 512],
                                 s.x2win[:, v * D + 512:(v + 1) * D], 2), "VE")
            ve2 = cnt["VE"]
            @block.scalar
            def _(scalar, ve2=ve2):
                w(scalar, "VE", ve2)
                inc(self.stats_s(scalar, 2), "SE")
            se2 = cnt["SE"]
            @block.vector
            def _(vector, v=v, se2=se2):
                w(vector, "SE", se2)
                self.chain(vector.tensor_scalar(out=un[:, :], in0=s.x2win[:, v * D:(v + 1) * D],
                                                scalar1=s.sA[:, 2:3], scalar2=None,
                                                op0=ALU.subtract), vector, "VE")
                self.chain(vector.reduce_max(out=s.sA[:, 4:5], in_=un[:, :], axis=AX.X,
                                             apply_absolute_value=True), vector, "VE")
                self.chain(vector.tensor_scalar(out=s.sA[:, 4:5], in0=s.sA[:, 4:5],
                                                scalar1=s.sA[:, 3:4], scalar2=1e-5,
                                                op0=ALU.mult, op1=ALU.max), vector, "VE")
                vector.tensor_scalar(out=s.sxcols[:, v:v + 1], in0=s.sA[:, 4:5],
                                     scalar1=1.0 / 127.0, scalar2=None, op0=ALU.mult)
                self.chain(vector.reciprocal(out=s.sA[:, 5:6], in_=s.sA[:, 4:5]), vector, "VE")
                self.chain(vector.tensor_scalar(out=s.sA[:, 6:7], in0=s.sA[:, 5:6],
                                                scalar1=s.sA[:, 3:4], scalar2=127.0,
                                                op0=ALU.mult, op1=ALU.mult), vector, "VE")
                vector.tensor_scalar(out=un[:, :], in0=un[:, :], scalar1=s.sA[:, 6:7],
                                     scalar2=128.0, op0=ALU.mult, op1=ALU.add)
                inc(self.emit_round_ladder(vector, un[:, :], ni[:, :], nf[:, :], gf[:, :],
                                           qib[:, :], -128), "VE")
            ve3 = cnt["VE"]
            if self.debug == "phg0":
                @block.sync
                def _(sync, v=v, ve3=ve3):
                    w(sync, "VE", ve3)
                    self.dinc(sync.dma_start(out=self._dbg_un[v * 128:(v + 1) * 128, :],
                                             in_=un[:, :]), "ST")
            @block.tensor
            def _(tensor, v=v, pp=pp, ve3=ve3):
                w(tensor, "VE", ve3)
                if v >= 2:
                    w(tensor, "VE")
                for dc in range(DC):
                    tensor.transpose(psQt[:, pp * 1024 + dc * 128:pp * 1024 + (dc + 1) * 128],
                                     qib[:, dc * 128:(dc + 1) * 128], s.idb[:, :])
                inc(tensor.transpose(psRw[0:1, 0:128], s.sxcols[:, v:v + 1], s.idf[:, :]), "TE")
            te2 = cnt["TE"]
            @block.vector
            def _(vector, v=v, pp=pp, te2=te2):
                w(vector, "TE", te2)
                vector.tensor_copy(out=qT_v[:, :, v * 128:(v + 1) * 128],
                                   in_=psQt[:, pp * 1024:(pp + 1) * 1024]
                                   .rearrange("p (dc f) -> p dc f", f=128))
                inc(vector.tensor_copy(out=sxrow[0:1, v * 128:(v + 1) * 128],
                                       in_=psRw[0:1, 0:128]), "VE")
        veF = cnt["VE"]
        for qi, (q0, qn) in enumerate(QCH):
            @block.tensor
            def _(tensor, q0=q0, qn=qn, qi=qi, veF=veF):
                if qi == 0:
                    w(tensor, "VE", veF)
                else:
                    w(tensor, "SE")
                inc(tensor.matmul(psRw[:, 0:qn], s.onesf1[:, :], sxrow[0:1, q0:q0 + qn],
                                  start=True, stop=True), "TE")
            te3 = cnt["TE"]
            @block.scalar
            def _(scalar, q0=q0, qn=qn, te3=te3):
                w(scalar, "TE", te3)
                inc(scalar.copy(out=s.sxinvb[:, q0:q0 + qn], in_=psRw[:, 0:qn]), "SE")
        st0.close()

    def ph_g(self, g0_only=False):
        s, nc, block = self, self.nc, self.block
        cnt, w, inc, dinc = self.cnt, self.w, self.inc, self.dinc
        stg = contextlib.ExitStack()
        s.qT = self.sb("qT", [128, DC * ETK], BF16, stack=stg)
        s.sxinvb = self.sb("sxinvb", [128, ETK], F32, stack=stg)
        s.sxcols = self.sb("sxcols", [128, ENT + 1], F32, stack=stg)
        if self.debug == "phg0":
            self._dbg_un = nc.dram_tensor("d_un", [ETK, D], F32, kind="ExternalOutput").ap()
            self.dbg_outs.append("d_un")
        self.ph_g0(stg)
        if g0_only:
            d1 = nc.dram_tensor("d_qT", [128, DC * ETK], BF16, kind="ExternalOutput").ap()
            pass
            d2_ = nc.dram_tensor("d_sxinv", [128, ETK], F32, kind="ExternalOutput").ap()
            self.dbg_outs += ["d_qT", "d_sxinv"]
            self.dbg_store([(d1[:, :], s.qT[:, :]), (d2_[:, :], s.sxinvb[:, :])])
            stg.close()
            return
        w1s = self.sb("w1s", [128, 2 * 1024], BF16, stack=stg)
        w2s = self.sb("w2s", [128, 2 * 1024], BF16, stack=stg)
        hcf = self.sb("hcf", [128, 390], F32, stack=stg)
        cvA = self.sb("cvA", [128, 386], F32, stack=stg)
        h2sb = self.sb("h2sb", [128, 386], F32, stack=stg)
        h2f = self.sb("h2f", [128, NHC * 384], BF16, stack=stg)
        h2q = self.sb("h2q", [128, NHC * 384], BF16, stack=stg)
        mxc = self.sb("mxc", [128, 384], F32, stack=stg)
        s2b = self.sb("s2b", [128, 384], F32, stack=stg)
        u2 = self.sb("u2", [128, 768], F32, stack=stg)
        n2i = self.sb("n2i", [128, 768], I32, stack=stg)
        n2f = self.sb("n2f", [128, 768], F32, stack=stg)
        g2f = self.sb("g2f", [128, 768], F32, stack=stg)
        srow = self.sb("srow", [1, 390], F32, stack=stg)
        inv2c = self.sb("inv2c", [128, 6], F32, stack=stg)
        ffo = self.sb("ffo", [128, D], F32, stack=stg)
        dl3 = self.sb("dl3", [128, D], F32, stack=stg)
        x4t = self.sb("x4t", [128, D], F32, stack=stg)

        psH = self.psum("psH", [128, 1024], F32, stack=stg)
        teH_h = {}
        geH_h = {}
        veH_h = {}
        teMo_h = {}
        mm2te_h = {}
        veQ_h = {}
        epi_ve_h = {}
        seS2b_h = {}

        def emit_mm1_conv_scale(si):
            t0, ntl = SEG[si]
            a = t0 * 128
            ntk = ntl * 128
            lo = max(a - 1, 0)
            hi = min(a + ntk + 1, ETK)
            ml = hi - lo
            pad_l = 1 if a == 0 else 0
            se_base = cnt["SE"]
            ge_base = cnt["GE"]
            veH_l = [None] * NHC
            geC_l = [None] * NHC
            for hc in range(NHC):
                pp = hc % 2
                @block.sync
                def _(sync, si=si, hc=hc, pp=pp):
                    if si == 0 and hc == 0:
                        self.gate(sync)
                    k = si * NHC + hc
                    if k >= 2:
                        w(sync, "TE", teH_h[(si, hc - 2) if hc >= 2 else
                                            (si - 1, NHC + hc - 2)])
                    for dc in range(DC):
                        dinc(sync.dma_start(
                            out=w1s[:, pp * 1024 + dc * 128:pp * 1024 + (dc + 1) * 128],
                            in_=s.scr_w1[hc * DC + dc:hc * DC + dc + 1, :]
                            .rearrange("a (p f) -> (a p) f", p=128)), "LD")
                ldg = self.dsnap("LD")
                @block.tensor
                def _(tensor, si=si, hc=hc, pp=pp, lo=lo, ml=ml, ldg=ldg):
                    self.dwait(tensor, "LD", ldg)
                    k = si * NHC + hc
                    if k >= 2:
                        w(tensor, "VE", veH_h[(si, hc - 2) if hc >= 2 else
                                              (si - 1, NHC + hc - 2)])
                    if hc <= 1 and si >= 1:
                        w(tensor, "SE", seS2b_h[si - 1])
                    bk = hc % 2
                    for dc in range(DC):
                        i = tensor.matmul(psH[:, bk * 512:bk * 512 + ml],
                                          w1s[:, pp * 1024 + dc * 128:pp * 1024 + (dc + 1) * 128],
                                          s.qT[:, dc * ETK + lo:dc * ETK + lo + ml],
                                          start=(dc == 0), stop=(dc == DC - 1))
                    inc(i, "TE")
                teH_h[(si, hc)] = cnt["TE"]
                teH = cnt["TE"]
                @block.vector
                def _(vector, si=si, hc=hc, lo=lo, ml=ml, ntk=ntk, pad_l=pad_l, teH=teH):
                    w(vector, "TE", teH)
                    if si == 0 and hc == 0:
                        w(vector, "SE")
                    bk = hc % 2
                    if pad_l:
                        vector.memset(hcf[:, 0:1], 0.0)
                    if pad_l + ml < ntk + 2:
                        vector.memset(hcf[:, pad_l + ml:ntk + 2], 0.0)
                    inc(vector.scalar_tensor_tensor(out=hcf[:, pad_l:pad_l + ml],
                                                    in0=psH[:, bk * 512:bk * 512 + ml],
                                                    scalar=s.wneg[:, 0:1],
                                                    in1=s.sxinvb[:, lo:lo + ml],
                                                    op0=ALU.mult, op1=ALU.mult), "VE")
                veH_h[(si, hc)] = cnt["VE"]
                veH_l[hc] = cnt["VE"]
                seC = se_base + 2 * hc + 1
                @block.vector
                def _(vector, hc=hc, ntk=ntk, seC=seC):
                    w(vector, "SE", seC)
                    inc(vector.scalar_tensor_tensor(out=cvA[:, 0:ntk], in0=hcf[:, 0:ntk],
                                                    scalar=s.cw_g[:, hc:hc + 1],
                                                    in1=cvA[:, 0:ntk],
                                                    op0=ALU.mult, op1=ALU.add), "VE")
                @block.vector
                def _(vector, hc=hc, ntk=ntk):
                    inc(vector.scalar_tensor_tensor(out=cvA[:, 0:ntk], in0=hcf[:, 2:2 + ntk],
                                                    scalar=s.cw_g[:, 2 * NHC + hc:2 * NHC + hc + 1],
                                                    in1=cvA[:, 0:ntk],
                                                    op0=ALU.mult, op1=ALU.add), "VE")
                geC_l[hc] = cnt["VE"]
                seS = se_base + 2 * hc + 2
                @block.gpsimd
                def _(g, si=si, hc=hc, ntk=ntk, seS=seS):
                    w(g, "SE", seS)
                    inc(g.engine_nop(), "GE")
                @block.vector
                def _(vector, hc=hc, ntk=ntk, seS=seS):
                    w(vector, "SE", seS)
                    if hc == 0:
                        vector.scalar_tensor_tensor(out=mxc[:, 0:ntk],
                                                    in0=h2sb[:, 0:ntk],
                                                    scalar=-1.0, in1=h2sb[:, 0:ntk],
                                                    op0=ALU.mult, op1=ALU.max)
                    else:
                        vector.scalar_tensor_tensor(out=mxc[:, 0:ntk], in0=h2sb[:, 0:ntk],
                                                    scalar=-1.0, in1=mxc[:, 0:ntk],
                                                    op0=ALU.mult, op1=ALU.max)
                        vector.tensor_tensor(out=mxc[:, 0:ntk], in0=mxc[:, 0:ntk],
                                             in1=h2sb[:, 0:ntk], op=ALU.max)
                    inc(vector.tensor_copy(out=h2f[:, hc * 384:hc * 384 + ntk],
                                           in_=h2sb[:, 0:ntk]), "VE")
            @block.scalar
            def _(scalar, ntk=ntk, veH_l=tuple(veH_l), geC_l=tuple(geC_l),
                  ge_base=ge_base):
                for hc in range(NHC):
                    w(scalar, "VE", veH_l[hc])
                    inc(scalar.activation(out=cvA[:, 0:ntk], in_=hcf[:, 1:1 + ntk],
                                          func=AF.Identity,
                                          scale=s.cw_g[:, NHC + hc:NHC + hc + 1],
                                          bias=s.cb_g[:, hc:hc + 1]), "SE")
                    w(scalar, "VE", geC_l[hc])
                    if hc >= 1:
                        w(scalar, "GE", ge_base + hc)
                    inc(scalar.activation(out=h2sb[:, 0:ntk], in_=cvA[:, 0:ntk],
                                          func=AF.Silu), "SE")
            assert cnt["SE"] == se_base + 2 * NHC
            veM = cnt["VE"]
            geM = ge_base + NHC
            geH_h[si] = geM
            iv = (si % 2) * 3
            @block.tensor
            def _(tensor, ntl=ntl, veM=veM, geM=geM):
                w(tensor, "VE", veM)
                w(tensor, "GE", geM)
                for k in range(ntl):
                    i = tensor.transpose(psH[:, k * 128:(k + 1) * 128],
                                         mxc[:, k * 128:(k + 1) * 128], s.idf[:, :])
                inc(i, "TE")
            teM = cnt["TE"]
            @block.vector
            def _(vector, ntl=ntl, teM=teM, iv=iv):
                w(vector, "TE", teM)
                for k in range(ntl):
                    self.chain(vector.reduce_max(out=s.sA[:, 8 + k:9 + k],
                                                 in_=psH[:, k * 128:(k + 1) * 128], axis=AX.X),
                               vector, "VE")
                    self.chain(vector.tensor_scalar(out=inv2c[:, iv + k:iv + k + 1],
                                                    in0=s.sA[:, 8 + k:9 + k],
                                                    scalar1=1e-5, scalar2=1.0 / 127.0,
                                                    op0=ALU.max, op1=ALU.mult), vector, "VE")
                    i = vector.reciprocal(out=s.sA[:, 12 + k:13 + k],
                                          in_=inv2c[:, iv + k:iv + k + 1])
                inc(i, "VE")
            veS2 = cnt["VE"]
            @block.tensor
            def _(tensor, ntl=ntl, veS2=veS2):
                w(tensor, "VE", veS2)
                for k in range(ntl):
                    i = tensor.transpose(psH[0:1, 512 + k * 128:512 + (k + 1) * 128],
                                         s.sA[:, 12 + k:13 + k], s.idf[:, :])
                inc(i, "TE")
            teS2 = cnt["TE"]
            @block.vector
            def _(vector, ntl=ntl, teS2=teS2):
                w(vector, "TE", teS2)
                inc(vector.tensor_copy(out=srow[0:1, 0:ntl * 128],
                                       in_=psH[0:1, 512:512 + ntl * 128]), "VE")
            veS3 = cnt["VE"]
            @block.tensor
            def _(tensor, ntk=ntk, veS3=veS3):
                w(tensor, "VE", veS3)
                inc(tensor.matmul(psH[:, 0:ntk], s.onesf1[:, :], srow[0:1, 0:ntk],
                                  start=True, stop=True), "TE")
            teS3 = cnt["TE"]
            @block.scalar
            def _(scalar, ntk=ntk, teS3=teS3):
                w(scalar, "TE", teS3)
                inc(scalar.copy(out=s2b[:, 0:ntk], in_=psH[:, 0:ntk]), "SE")
            seS2b_h[si] = cnt["SE"]

        def emit_quant(si):
            t0, ntl = SEG[si]
            ntk = ntl * 128
            h2f_v = h2f.rearrange("p (hc f) -> p hc f", f=384)
            h2q_v = h2q.rearrange("p (hc f) -> p hc f", f=384)
            for hp in range(NHC // 2):
                hc = hp * 2
                def emitq(e, hc=hc, ntk=ntk):
                    e.tensor_tensor(out=u2[:, :].rearrange("p (a f) -> p a f", a=2)[:, :, 0:ntk],
                                    in0=h2f_v[:, hc:hc + 2, 0:ntk],
                                    in1=s2b[:, 0:ntk].rearrange("p f -> p () f")
                                    .broadcast_to((128, 2, ntk)), op=ALU.mult)
                    e.tensor_scalar(out=u2[:, :], in0=u2[:, :],
                                    scalar1=128.0, scalar2=None, op0=ALU.add)
                    dst = h2q_v[:, hc:hc + 2, 0:ntk]
                    e.tensor_copy(out=n2i[:, :], in_=u2[:, :])
                    e.tensor_copy(out=n2f[:, :], in_=n2i[:, :])
                    e.tensor_tensor(out=g2f[:, :], in0=u2[:, :], in1=n2f[:, :],
                                    op=ALU.subtract)
                    nfv = n2f[:, :].rearrange("p (a f) -> p a f", a=2)[:, :, 0:ntk]
                    gv = g2f[:, :].rearrange("p (a f) -> p a f", a=2)[:, :, 0:ntk]
                    e.tensor_scalar(out=dst, in0=nfv, scalar1=-131.0, scalar2=None,
                                    op0=ALU.add)
                    for c in (-2.5, -1.5, -0.5, 0.5, 1.5, 2.5):
                        i = e.scalar_tensor_tensor(out=dst, in0=gv, scalar=c,
                                                   in1=dst, op0=ALU.is_ge, op1=ALU.add)
                    return i
                @block.vector
                def _(vector, si=si, hp=hp):
                    if hp == 0:
                        w(vector, "SE", seS2b_h[si])
                        w(vector, "GE", geH_h[si])
                        if si >= 1:
                            w(vector, "TE", teMo_h[si - 1])
                    inc(emitq(vector), "VE")
            veQ_h[si] = cnt["VE"]

        def emit_mm2(si):
            t0, ntl = SEG[si]
            iv = (si % 2) * 3
            stg2 = contextlib.ExitStack()
            psMo = self.psum("psMo_%d" % si, [128, ntl * 1024], F32, stack=stg2)
            for hc in range(NHC):
                pp = hc % 2
                @block.sync
                def _(sync, si=si, hc=hc, pp=pp):
                    if si == 0 and hc == 0:
                        self.dwait(sync, "SD")
                    k = si * NHC + hc
                    if k >= 2:
                        w(sync, "TE", mm2te_h[(si, hc - 2) if hc >= 2 else
                                              (si - 1, NHC + hc - 2)])
                    dinc(sync.dma_start(out=w2s[:, pp * 1024:(pp + 1) * 1024],
                                       in_=s.scr_w2[hc * 128:(hc + 1) * 128, :]), "LD")
                ldg2 = self.dsnap("LD")
                @block.tensor
                def _(tensor, si=si, hc=hc, pp=pp, ntl=ntl, ldg2=ldg2):
                    self.dwait(tensor, "LD", ldg2)
                    if hc == 0:
                        w(tensor, "VE", veQ_h[si])
                        if si >= 1:
                            w(tensor, "VE", epi_ve_h[si - 1])
                    for k in range(ntl):
                        for hf2 in range(2):
                            i = tensor.matmul(
                                psMo[:, k * 1024 + hf2 * 512:k * 1024 + (hf2 + 1) * 512],
                                h2q[:, hc * 384 + k * 128:hc * 384 + k * 128 + 128],
                                w2s[:, pp * 1024 + hf2 * 512:pp * 1024 + (hf2 + 1) * 512],
                                start=(hc == 0), stop=(hc == NHC - 1))
                    inc(i, "TE")
                mm2te_h[(si, hc)] = cnt["TE"]
            teMo_h[si] = cnt["TE"]
            teMo = cnt["TE"]
            for k in range(ntl):
                v = t0 + k
                @block.vector
                def _(vector, k=k, teMo=teMo, iv=iv):
                    w(vector, "TE", teMo)
                    inc(vector.tensor_scalar(out=ffo[:, :],
                                             in0=psMo[:, k * 1024:(k + 1) * 1024],
                                             scalar1=inv2c[:, iv + k:iv + k + 1],
                                             scalar2=s.wneg[:, 1:2],
                                             op0=ALU.mult, op1=ALU.mult), "VE")
                vex = cnt["VE"]
                @block.vector
                def _(vector, vex=vex):
                    w(vector, "VE", vex)
                    inc(self.stats_v(vector, ffo[:, 0:512], ffo[:, 512:1024], 0), "VE")
                ve1 = cnt["VE"]
                @block.scalar
                def _(scalar, ve1=ve1):
                    w(scalar, "VE", ve1)
                    inc(self.stats_s(scalar, 0), "SE")
                se1 = cnt["SE"]
                @block.vector
                def _(vector, v=v, se1=se1):
                    w(vector, "SE", se1)
                    self.dwait(vector, "ST")
                    vector.tensor_scalar(out=dl3[:, :], in0=ffo[:, :], scalar1=s.sA[:, 0:1],
                                         scalar2=s.sA[:, 1:2], op0=ALU.subtract, op1=ALU.mult)
                    inc(vector.tensor_tensor(out=x4t[:, :],
                                             in0=s.x2win[:, v * D:(v + 1) * D],
                                             in1=dl3[:, :], op=ALU.add), "VE")
                vey = cnt["VE"]
                @block.gpsimd
                def _(g, v=v, vey=vey):
                    w(g, "VE", vey)
                    dinc(g.dma_start(out=s.y[v * 128:(v + 1) * 128, :],
                                    in_=x4t[:, :]), "ST")
            epi_ve_h[si] = cnt["VE"]
            stg2.close()

        for si in range(len(SEG)):
            emit_mm1_conv_scale(si)
            if si >= 1:
                emit_mm2(si - 1)
            emit_quant(si)
        emit_mm2(len(SEG) - 1)
        stg.close()

    def fin(self):
        for st in reversed(self._open_stacks):
            st.close()
        block = self.block
        w = self.w
        @block.sync
        def _(sync):
            for nm in ("VE", "SE", "TE", "GE"):
                w(sync, nm)
            for nm in ("ST", "GD", "SD", "LD"):
                self.dwait(sync, nm)
        @block.gpsimd
        def _(g):
            self.dwait(g, "ST")
        self.stack.close()
        return self

import numpy as np
from concourse.bass_utils import run_bass_kernel_spmd

_CACHED = {}


def _build_cached():
    if "m" not in _CACHED:
        _CACHED["m"] = MK(debug=None).build()
    return _CACHED["m"]


def _host_inputs(inp, b, j):
    f32 = np.float32
    d = {}
    d["xb"] = np.ascontiguousarray(inp["L0_attn_x"][b]).astype(f32)
    d["jsel"] = np.array([[j, 1 - j]], dtype=f32)
    d["identf"] = np.eye(128, dtype=f32)
    import ml_dtypes
    d["identb"] = np.eye(128).astype(ml_dtypes.bfloat16)
    d["WiT"] = np.ascontiguousarray(inp["L0_attn_Wi"].T)
    d["WrT"] = np.ascontiguousarray(inp["L0_attn_Wr"].T)
    d["bi_pp"] = np.ascontiguousarray(inp["L0_attn_bi"].reshape(DC, 128).T)
    d["br_pp"] = np.ascontiguousarray(inp["L0_attn_br"].reshape(DC, 128).T)
    d["lam"] = np.asarray(inp["L0_attn_cap_lambda"]).reshape(1, 1)
    d["Win"] = np.asarray(inp["L0_ffn_Win"])
    d["binr"] = np.asarray(inp["L0_ffn_bin_"]).reshape(1, 64)
    d["W1r"] = np.ascontiguousarray(np.asarray(inp["L0_ffn_W1"]).reshape(1, 1024))
    d["b1r"] = np.ascontiguousarray(np.asarray(inp["L0_ffn_b1"]).reshape(1, 1024))
    d["W2r"] = np.ascontiguousarray(np.asarray(inp["L0_ffn_W2"]).reshape(1, 1024))
    d["b2r"] = np.asarray(inp["L0_ffn_b2"]).reshape(1, 64)
    d["Wout"] = np.asarray(inp["L0_ffn_Wout"])
    d["boutr"] = np.asarray(inp["L0_ffn_bout"]).reshape(1, D)
    d["Wq"] = np.asarray(inp["L1_attn_Wq"])
    d["Wk_up"] = np.asarray(inp["L1_attn_Wk_up"])
    d["Wv_up"] = np.asarray(inp["L1_attn_Wv_up"])
    d["WoT"] = np.ascontiguousarray(np.asarray(inp["L1_attn_Wo"]).T)
    d["WkvdT"] = np.ascontiguousarray(np.asarray(inp["L1_attn_Wkv_down"]).T)
    d["w1T"] = np.ascontiguousarray(np.asarray(inp["L1_ffn_w1"]).T)
    d["w2T"] = np.ascontiguousarray(np.asarray(inp["L1_ffn_w2"]).T)
    cw = np.asarray(inp["L1_ffn_conv_w"])[:, 0, :]
    d["cwr"] = np.ascontiguousarray(
        np.concatenate([cw[:, k].reshape(NHC, 128).T for k in range(3)], axis=1))
    d["cbr"] = np.ascontiguousarray(np.asarray(inp["L1_ffn_conv_b"]).reshape(NHC, 128).T)
    return {k: np.ascontiguousarray(np.asarray(v, dtype=np.float32)
                                    if v.dtype == np.float64 else v)
            for k, v in d.items()}


def kernel(**inputs):
    m = _build_cached()
    ins = [_host_inputs(inputs, c // 2, c % 2) for c in range(8)]
    res = run_bass_kernel_spmd(m.nc, ins, core_ids=list(range(8)))
    out = np.empty((4, T, D), dtype=np.float32)
    for c in range(8):
        b, j = c // 2, c % 2
        yv = np.asarray(res.results[c]["y"])
        off = 0 if j == 0 else 128
        out[b, j * 2048:(j + 1) * 2048] = yv[off:off + 2048]
    return out


# version 11 (final)
# speedup vs baseline: 1.0344x; 1.0344x over previous
import contextlib
from collections import defaultdict

import concourse.bass as bass
import concourse.mybir as mybir

F32 = mybir.dt.float32
BF16 = mybir.dt.bfloat16
I32 = mybir.dt.int32
AF = mybir.ActivationFunctionType
ALU = mybir.AluOpType
AX = mybir.AxisListType

T = 4096
D = 1024
NT = 32
DC = 8
ENT = 17
ETK = ENT * 128
H = 8
LAT = 64
HID = 4096
NHC = 32
EPS = 1e-5
SCALE = 1.0 / 16.0
ESH = -3.0
QCH = [(0, 512), (512, 512), (1024, 512), (1536, 512), (2048, 128)]
SEG = [(0, 3), (3, 3), (6, 3), (9, 3), (12, 3), (15, 2)]


class MK:
    def __init__(self, debug=None):
        self.debug = debug
        self.nc = bass.Bass("TRN2", target_bir_lowering=False, detect_race_conditions=False)
        self.cnt = defaultdict(int)
        self.didx = defaultdict(int)
        self.dcnt = {b: [0, 0, 0, 0] for b in ("LD", "GD", "SD", "ST")}
        self.sems = {}
        self.stack = contextlib.ExitStack()
        self._open_stacks = []
        self.dbg_outs = []

    def sem(self, name):
        self.sems[name] = self.stack.enter_context(self.nc.semaphore(name))

    def dinc(self, inst, base, n=16):
        k = self.didx[base] % 4
        self.didx[base] += 1
        inst.then_inc(self.sems[base + str(k)], n)
        self.dcnt[base][k] += n

    def dsnap(self, base):
        return tuple(self.dcnt[base])

    def dwait(self, eng, base, snap=None):
        snap = tuple(self.dcnt[base]) if snap is None else snap
        for k in range(4):
            if snap[k] > 0:
                eng.wait_ge(self.sems[base + str(k)], snap[k])

    def inc(self, inst, name, n=1):
        inst.then_inc(self.sems[name], n)
        self.cnt[name] += n

    def w(self, eng, name, val=None):
        v = self.cnt[name] if val is None else val
        if v > 0:
            eng.wait_ge(self.sems[name], v)

    def sb(self, name, shape, dt=F32, stack=None):
        return (stack or self.stack).enter_context(self.nc.sbuf_tensor(name, shape, dt))

    def psum(self, name, shape, dt=F32, stack=None):
        return (stack or self.stack).enter_context(self.nc.psum_tensor(name, shape, dt))

    def emit_round_ladder(self, vector, t_ap, ni_ap, nf_ap, v_ap, out_ap, off):
        vector.tensor_copy(out=ni_ap, in_=t_ap)
        vector.tensor_copy(out=nf_ap, in_=ni_ap)
        vector.tensor_tensor(out=v_ap, in0=t_ap, in1=nf_ap, op=ALU.subtract)
        vector.tensor_scalar(out=out_ap, in0=nf_ap, scalar1=float(off - 3),
                             scalar2=None, op0=ALU.add)
        for c in (-2.5, -1.5, -0.5, 0.5, 1.5, 2.5):
            i = vector.scalar_tensor_tensor(out=out_ap, in0=v_ap, scalar=c,
                                            in1=out_ap, op0=ALU.is_ge, op1=ALU.add)
        return i

    def chain(self, inst, eng, name):
        self.inc(inst, name)
        self.w(eng, name)

    def gate(self, eng, dma=True):
        for nm in ("TE", "VE", "SE", "GE"):
            self.w(eng, nm)
        if dma:
            for b in ("LD", "GD", "SD", "ST"):
                self.dwait(eng, b)

    def dbg_store(self, pairs):
        block = self.block
        w, inc = self.w, self.inc
        dinc = self.dinc
        @block.sync
        def _(sync):
            for nm in ("VE", "SE", "TE", "GE"):
                w(sync, nm)
            for dst, src in pairs:
                dinc(sync.dma_start(out=dst, in_=src), "ST")
            self.dwait(sync, "ST")

    def stats_v(self, vector, src_ap0, src_ap1, c):
        sA, stats = self.sA, self.stats
        vector.bn_stats(out=stats[:, 0:6], in_=src_ap0)
        self.chain(vector.bn_stats(out=stats[:, 6:12], in_=src_ap1), vector, "VE")
        self.chain(vector.bn_aggr(out=sA[:, c:c + 2], in_=stats[:, :]), vector, "VE")
        self.chain(vector.tensor_scalar(out=sA[:, c + 1:c + 2], in0=sA[:, c + 1:c + 2],
                                        scalar1=EPS, scalar2=None, op0=ALU.add), vector, "VE")
        return vector.reciprocal(out=sA[:, c + 1:c + 2], in_=sA[:, c + 1:c + 2])

    def stats_s(self, scalar, c):
        return scalar.activation(out=self.sA[:, c + 1:c + 2], in_=self.sA[:, c + 1:c + 2],
                                 func=AF.Sqrt)

    def build(self):
        nc = self.nc
        debug = self.debug
        din = lambda name, shape, dt=F32: nc.dram_tensor(name, shape, dt, kind="ExternalInput").ap()

        s = self
        s.xb = din("xb", [T, D])
        s.jsel = din("jsel", [1, 2])
        s.identf = din("identf", [128, 128], F32)
        s.identb = din("identb", [128, 128], BF16)
        s.WiT = din("WiT", [D, D])
        s.WrT = din("WrT", [D, D])
        s.bi_pp = din("bi_pp", [128, DC])
        s.br_pp = din("br_pp", [128, DC])
        s.lam = din("lam", [1, 1])
        s.Win = din("Win", [D, 64])
        s.binr = din("binr", [1, 64])
        s.W1r = din("W1r", [1, 1024])
        s.b1r = din("b1r", [1, 1024])
        s.W2r = din("W2r", [1, 1024])
        s.b2r = din("b2r", [1, 64])
        s.Wout = din("Wout", [64, D])
        s.boutr = din("boutr", [1, D])
        s.Wq = din("Wq", [2048, D])
        s.Wk_up = din("Wk_up", [2048, LAT])
        s.Wv_up = din("Wv_up", [2048, LAT])
        s.WoT = din("WoT", [2048, D])
        s.WkvdT = din("WkvdT", [D, LAT])
        s.w1T = din("w1T", [D, HID])
        s.w2T = din("w2T", [HID, D])
        s.cwr = din("cwr", [128, 3 * NHC])
        s.cbr = din("cbr", [128, NHC])

        s.y = nc.dram_tensor("y", [ETK, D], F32, kind="ExternalOutput").ap()
        s.scr_w1 = nc.dram_tensor("scr_w1", [NHC * DC, 128 * 128], BF16).ap()
        s.scr_w2 = nc.dram_tensor("scr_w2", [HID, D], BF16).ap()
        s.scr_x2 = nc.dram_tensor("scr_x2", [T, D], F32).ap()

        for nm in ("TE", "VE", "SE", "GE"):
            self.sem(nm)
        for b in ("LD", "GD", "SD", "ST"):
            for k in range(4):
                self.sem(b + str(k))

        sb = self.sb
        s.idf = sb("idf", [128, 128], F32)
        s.idb = sb("idb", [128, 128], BF16)
        s.onesf1 = sb("onesf1", [1, 128], F32)
        s.onescol = sb("onescol", [128, 1], F32)
        s.jv = sb("jv", [128, 2], F32)
        s.c8v = sb("c8v", [128, 1], F32)
        s.eshv = sb("eshv", [128, 1], F32)
        s.cw_g = sb("cw_g", [128, 3 * NHC], F32)
        s.cb_g = sb("cb_g", [128, NHC], F32)
        s.kvdb = sb("kvdb", [128, DC * 128], BF16)
        s.wneg = sb("wneg", [128, 2], F32)
        s.wtau = sb("wtau", [128, 4], F32)
        s.t2 = sb("t2", [1, 16], F32)
        s.stats = sb("stats", [128, 12], F32)
        s.sA = sb("sA", [128, 16], F32)
        st_l0c = contextlib.ExitStack()
        self._open_stacks.append(st_l0c)
        self._st_l0c = st_l0c
        s.bi_s = sb("bi_s", [128, DC], F32, stack=st_l0c)
        s.br_s = sb("br_s", [128, DC], F32, stack=st_l0c)
        s.winb = sb("winb", [128, DC * 64], BF16, stack=st_l0c)
        s.cswb = sb("cswb", [128, 64], F32, stack=st_l0c)
        s.binb = sb("binb", [128, 64], BF16, stack=st_l0c)
        s.W1b = sb("W1b", [128, 1024], BF16, stack=st_l0c)
        s.b1b = sb("b1b", [128, 1024], BF16, stack=st_l0c)
        s.W2b = sb("W2b", [128, 1024], BF16, stack=st_l0c)
        s.b2b = sb("b2b", [128, 64], BF16, stack=st_l0c)
        s.woutb = sb("woutb", [64, D], BF16, stack=st_l0c)
        s.boutb = sb("boutb", [128, D], BF16, stack=st_l0c)

        self.block = self.stack.enter_context(nc.Block())

        self.ph_0()
        if debug == "ph0":
            self.ph_w_pass1()
            self.ph_w_finish()
            dd = nc.dram_tensor("d_misc", [128, 96], F32, kind="ExternalOutput").ap()
            self.dbg_outs.append("d_misc")
            dmx = sb("dmx", [128, 96], F32)
            block = self.block
            @block.vector
            def _(vector):
                self.gate(vector)
                vector.memset(dmx[:, :], 0.0)
                vector.tensor_copy(out=dmx[:, 0:64], in_=s.cswb[:, :])
                vector.tensor_copy(out=dmx[:, 64:65], in_=s.c8v[:, :])
                vector.tensor_copy(out=dmx[:, 65:67], in_=s.wneg[:, :])
                vector.tensor_copy(out=dmx[:, 67:71], in_=s.wtau[:, :])
                vector.tensor_copy(out=dmx[0:1, 73:73 + 16], in_=s.t2[0:1, :])
                self.inc(vector.tensor_copy(out=dmx[:, 71:73], in_=s.jv[:, :]), "VE")
            self.dbg_store([(dd[:, :], dmx[:, :])])
            return self.fin()

        st_ysb = contextlib.ExitStack()
        self._open_stacks.append(st_ysb)
        s.ysb = sb("ysb", [128, NT * D], BF16, stack=st_ysb)
        st_a = contextlib.ExitStack()
        self._open_stacks.append(st_a)
        s.ln0T = sb("ln0T", [128, DC * T], BF16, stack=st_a)
        self.ph_w_pass1()
        self.ph_a()
        self.ph_w_finish()
        if debug == "pha":
            dd = nc.dram_tensor("d_ln0T", [128, DC * T], BF16, kind="ExternalOutput").ap()
            self.dbg_outs.append("d_ln0T")
            self.dbg_store([(dd[:, :], s.ln0T[:, :])])
            return self.fin()

        self.ph_b()
        st_a.close()
        if debug == "phb":
            dd = nc.dram_tensor("d_ysb", [128, NT * D], BF16, kind="ExternalOutput").ap()
            self.dbg_outs.append("d_ysb")
            self.dbg_store([(dd[:, :], s.ysb[:, :])])
            return self.fin()

        self.ph_c1()
        st_ysb.close()
        self._st_l0c.close()
        if debug == "phc1":
            dd = nc.dram_tensor("d_x2", [T, D], F32, kind="ExternalOutput").ap()
            self.dbg_outs.append("d_x2")
            block = self.block
            w, inc = self.w, self.inc
            @block.sync
            def _(sync):
                for nm in ("VE", "SE", "TE", "GE"):
                    w(sync, nm)
                self.dwait(sync, "SD")
            xl = sb("xl2", [128, D], F32)
            for tt in range(NT):
                @block.sync
                def _(sync, tt=tt):
                    if tt > 0:
                        self.dwait(sync, "ST")
                    self.dinc(sync.dma_start(out=xl[:, :],
                                             in_=s.scr_x2[tt * 128:(tt + 1) * 128, :]), "LD")
                    self.dwait(sync, "LD")
                    self.dinc(sync.dma_start(out=dd[tt * 128:(tt + 1) * 128, :], in_=xl[:, :]),
                              "ST")
            return self.fin()

        st_x = contextlib.ExitStack()
        self._open_stacks.append(st_x)
        s.x2win = sb("x2win", [128, ENT * D], F32, stack=st_x)
        st_de = contextlib.ExitStack()
        self._open_stacks.append(st_de)
        s.latT = sb("latT", [128, T], BF16, stack=st_de)
        s.qpT = sb("qpT", [128, 4 * ETK], BF16, stack=st_de)
        st_m = contextlib.ExitStack()
        self._open_stacks.append(st_m)
        s.Mbuf = sb("Mbuf", [128, DC * 512], BF16, stack=st_m)
        self.ph_mn(do_m=True, do_n=False)
        if debug == "phmn":
            s.Nbuf = sb("Nbuf", [128, 4 * D], BF16, stack=st_m)
            self.ph_mn(do_m=False, do_n=True)
            dm = nc.dram_tensor("d_M", [128, DC * 512], BF16, kind="ExternalOutput").ap()
            dn = nc.dram_tensor("d_N", [128, 4 * D], BF16, kind="ExternalOutput").ap()
            self.dbg_outs += ["d_M", "d_N"]
            self.dbg_store([(dm[:, :], s.Mbuf[:, :]), (dn[:, :], s.Nbuf[:, :])])
            return self.fin()

        st_c2 = contextlib.ExitStack()
        self._open_stacks.append(st_c2)
        s.ln2T = sb("ln2T", [128, DC * T], BF16, stack=st_c2)
        self.ph_c2()
        if debug == "phc2":
            d1 = nc.dram_tensor("d_ln2T", [128, DC * T], BF16, kind="ExternalOutput").ap()
            d2_ = nc.dram_tensor("d_x2win", [128, ENT * D], F32, kind="ExternalOutput").ap()
            self.dbg_outs += ["d_ln2T", "d_x2win"]
            self.dbg_store([(d1[:, :], s.ln2T[:, :]), (d2_[:, :], s.x2win[:, :])])
            return self.fin()

        st_d = contextlib.ExitStack()
        self._open_stacks.append(st_d)
        s.ln2Twin = sb("ln2Twin", [128, DC * 1152], BF16, stack=st_d)
        self.ph_d()
        st_d.close()
        st_c2.close()
        st_m.close()
        if debug == "phd":
            d1 = nc.dram_tensor("d_latT", [128, T], BF16, kind="ExternalOutput").ap()
            d2_ = nc.dram_tensor("d_qpT", [128, 4 * ETK], BF16, kind="ExternalOutput").ap()
            self.dbg_outs += ["d_latT", "d_qpT"]
            self.dbg_store([(d1[:, :], s.latT[:, :]), (d2_[:, :], s.qpT[:, :])])
            return self.fin()

        st_ef = contextlib.ExitStack()
        self._open_stacks.append(st_ef)
        s.oTn = sb("oTn", [128, 4 * ETK], BF16, stack=st_ef)
        st_n = contextlib.ExitStack()
        self._open_stacks.append(st_n)
        s.Nbuf = sb("Nbuf", [128, 4 * D], BF16, stack=st_n)
        st_e = contextlib.ExitStack()
        self._open_stacks.append(st_e)
        self.ph_e(st_e)
        st_e.close()
        if debug == "phe":
            d1 = nc.dram_tensor("d_oTn", [128, 4 * ETK], BF16, kind="ExternalOutput").ap()
            self.dbg_outs += ["d_oTn"]
            self.dbg_store([(d1[:, :], s.oTn[:, :])])
            return self.fin()

        self.ph_mn(do_m=False, do_n=True)
        st_f = contextlib.ExitStack()
        self._open_stacks.append(st_f)
        self.ph_f(st_f)
        st_f.close()
        if debug == "phf":
            d1 = nc.dram_tensor("d_x3", [128, ENT * D], F32, kind="ExternalOutput").ap()
            self.dbg_outs += ["d_x3"]
            self.dbg_store([(d1[:, :], s.x2win[:, :])])
            return self.fin()
        st_n.close()
        st_ef.close()
        st_de.close()

        if debug == "phg0":
            self.ph_g(g0_only=True)
            st_x.close()
            return self.fin()
        if debug == "phg1":
            self.ph_g()
            st_x.close()
            return self.fin()
        self.ph_g(g0_only=(debug == "phg0"))
        st_x.close()
        return self.fin()

    def ph_0(self):
        s, nc, block = self, self.nc, self.block
        cnt, w, inc, dinc = self.cnt, self.w, self.inc, self.dinc
        @block.sync
        def _(sync):
            for dst, src in ((s.idf, s.identf), (s.idb, s.identb),
                             (s.bi_s, s.bi_pp), (s.br_s, s.br_pp),
                             (s.cw_g, s.cwr), (s.cb_g, s.cbr)):
                dinc(sync.dma_start(out=dst[:, :], in_=src[:, :]), "LD")
            dinc(sync.dma_start(out=s.jv[:, :], in_=s.jsel.broadcast_to((128, 2))), "LD")
            dinc(sync.dma_start(out=s.t2[:, 0:1], in_=s.lam[:, :]), "LD")
        @block.gpsimd
        def _(g):
            dinc(g.dma_start(out=s.binb[:, :], in_=s.binr.broadcast_to((128, 64))), "GD")
            dinc(g.dma_start(out=s.W1b[:, :], in_=s.W1r.broadcast_to((128, 1024))), "GD")
            dinc(g.dma_start(out=s.b1b[:, :], in_=s.b1r.broadcast_to((128, 1024))), "GD")
            dinc(g.dma_start(out=s.W2b[:, :], in_=s.W2r.broadcast_to((128, 1024))), "GD")
            dinc(g.dma_start(out=s.b2b[:, :], in_=s.b2r.broadcast_to((128, 64))), "GD")
            dinc(g.dma_start(out=s.boutb[:, :], in_=s.boutr.broadcast_to((128, D))), "GD")
        self._gd_ph0 = self.dsnap("GD")
        ld0 = self.dsnap("LD")

        stp = contextlib.ExitStack()
        ps00 = self.psum("ps00", [128, 512], F32, stack=stp)
        wtm = self.sb("wtm", [128, 2048], F32, stack=stp)
        @block.sync
        def _(sync):
            self.dwait(sync, "LD")
            for dc in range(DC):
                dinc(sync.dma_start(out=wtm[:, dc * 64:dc * 64 + 64],
                                   in_=s.Win[dc * 128:(dc + 1) * 128, :]), "LD")
            dinc(sync.dma_start(out=wtm[0:64, 512:512 + D], in_=s.Wout[:, :]), "LD")
            for dc in range(DC):
                dinc(sync.dma_start(out=wtm[:, 1536 + dc * 64:1536 + (dc + 1) * 64],
                                   in_=s.WkvdT[dc * 128:(dc + 1) * 128, :]), "LD")
        ldw = self.dsnap("LD")
        @block.vector
        def _(vector):
            vector.memset(s.onesf1[:, :], 1.0)
            vector.memset(s.t2[:, 1:16], 0.0)
            vector.memset(s.eshv[:, :], ESH)
            vector.memset(s.sA[:, :], 0.0)
            inc(vector.memset(s.onescol[:, :], 1.0), "VE")
            self.dwait(vector, "LD", ldw)
            vector.tensor_copy(out=s.winb[:, :], in_=wtm[:, 0:512])
            vector.tensor_copy(out=s.woutb[:, :], in_=wtm[0:64, 512:512 + D])
            kv2 = s.kvdb.rearrange("p (dc two f) -> p dc two f", two=2, f=LAT)
            srcv = wtm[:, 1536:1536 + 512].rearrange("p (dc o f) -> p dc o f", o=1, f=LAT)
            vector.tensor_copy(out=kv2[:, :, 0:1, :], in_=srcv)
            inc(vector.tensor_copy(out=kv2[:, :, 1:2, :], in_=srcv), "VE")
        ve0 = cnt["VE"]
        @block.scalar
        def _(scalar):
            self.dwait(scalar, "LD", ld0)
            self.chain(scalar.activation(out=s.t2[:, 1:2], in_=s.t2[:, 0:1], func=AF.Exp),
                       scalar, "SE")
            self.chain(scalar.activation(out=s.t2[:, 2:3], in_=s.t2[:, 1:2], func=AF.Ln,
                                         bias=1.0), scalar, "SE")
            inc(scalar.mul(out=s.t2[:, 3:4], in_=s.t2[:, 2:3], mul=-8.0), "SE")
        se0 = cnt["SE"]
        @block.tensor
        def _(tensor):
            w(tensor, "VE", ve0)
            w(tensor, "SE", se0)
            tensor.matmul(ps00[:, 0:1], s.onesf1[:, :], s.t2[:, 3:4], start=True, stop=True)
            for dc in range(DC):
                i = tensor.matmul(ps00[0:1, 64:128], s.onescol[:, :],
                                  wtm[:, dc * 64:(dc + 1) * 64],
                                  start=(dc == 0), stop=(dc == DC - 1))
            inc(i, "TE")
        te0 = cnt["TE"]
        @block.scalar
        def _(scalar):
            w(scalar, "TE", te0)
            scalar.copy(out=s.c8v[:, :], in_=ps00[:, 0:1])
            inc(scalar.copy(out=wtm[0:1, 0:64], in_=ps00[0:1, 64:128]), "SE")
        se1 = cnt["SE"]
        @block.tensor
        def _(tensor):
            w(tensor, "SE", se1)
            inc(tensor.matmul(ps00[:, 64:128], s.onesf1[:, :], wtm[0:1, 0:64],
                              start=True, stop=True), "TE")
        te1 = cnt["TE"]
        @block.scalar
        def _(scalar):
            w(scalar, "TE", te1)
            inc(scalar.copy(out=s.cswb[:, :], in_=ps00[:, 64:128]), "SE")
        stp.close()

    def _w_piece_src(self, wi, c):
        s = self
        if wi == 0:
            dc, cp = c // 4, c % 4
            return s.w1T[dc * 128:(dc + 1) * 128, cp * 1024:(cp + 1) * 1024]
        return s.w2T[c * 128:(c + 1) * 128, :]

    def ph_w_pass1(self):
        s, nc, block = self, self.nc, self.block
        cnt, w, inc, dinc = self.cnt, self.w, self.inc, self.dinc
        self._stw = contextlib.ExitStack()
        self._wld = self.sb("wld", [128, 4 * 1024], F32, stack=self._stw)
        self._wq_a = self.sb("wq_a", [128, 1024], F32, stack=self._stw)
        self._wq_o = self.sb("wq_o", [128, 2 * 1024], BF16, stack=self._stw)
        self._wsum = self.sb("wsum", [128, 64], F32, stack=self._stw)
        wld, wsum = self._wld, self._wsum
        @block.gpsimd
        def _(g):
            w(g, "SE")
            w(g, "TE")
        self._se_base = cnt["SE"]
        snaps = []
        for k in range(64):
            wi, c = k // 32, k % 32
            @block.gpsimd
            def _(g, wi=wi, c=c, k=k):
                buf = wld[:, (k % 4) * 1024:(k % 4 + 1) * 1024]
                if k >= 4:
                    w(g, "SE", self._se_base + k - 3)
                dinc(g.dma_start(out=buf, in_=self._w_piece_src(wi, c)), "GD")
            snaps.append(self.dsnap("GD"))
        @block.scalar
        def _(scalar, snaps=tuple(snaps)):
            for k in range(64):
                self.dwait(scalar, "GD", snaps[k])
                inc(scalar.activation(out=self._wq_a[:, :],
                                      in_=wld[:, (k % 4) * 1024:(k % 4 + 1) * 1024],
                                      func=AF.Abs,
                                      accum_out=wsum[:, k:k + 1]), "SE")
        self._se_p1 = cnt["SE"]

    def ph_w_finish(self):
        s, nc, block = self, self.nc, self.block
        cnt, w, inc, dinc = self.cnt, self.w, self.inc, self.dinc
        wld, wsum, wq_a, wq_o = self._wld, self._wsum, self._wq_a, self._wq_o
        stw2 = contextlib.ExitStack()
        psw = self.psum("psw", [128, 512], F32, stack=stw2)
        @block.vector
        def _(vector):
            w(vector, "SE", self._se_p1)
            vector.reduce_sum(out=s.sA[:, 8:9], in_=wsum[:, 0:32], axis=AX.X)
            inc(vector.reduce_sum(out=s.sA[:, 9:10], in_=wsum[:, 32:64], axis=AX.X), "VE")
        vev = cnt["VE"]
        @block.tensor
        def _(tensor):
            w(tensor, "VE", vev)
            tensor.transpose(psw[0:1, 0:128], s.sA[:, 8:9], s.idf[:, :])
            inc(tensor.transpose(psw[0:1, 128:256], s.sA[:, 9:10], s.idf[:, :]), "TE")
        tev = cnt["TE"]
        @block.vector
        def _(vector):
            w(vector, "TE", tev)
            self.chain(vector.reduce_sum(out=s.t2[0:1, 8:9], in_=psw[0:1, 0:128], axis=AX.X),
                       vector, "VE")
            self.chain(vector.reduce_sum(out=s.t2[0:1, 9:10], in_=psw[0:1, 128:256],
                                         axis=AX.X), vector, "VE")
            numel = float(D * HID)
            self.chain(vector.tensor_scalar(out=s.t2[0:1, 10:12], in0=s.t2[0:1, 8:10],
                                            scalar1=1.0 / numel, scalar2=None, op0=ALU.mult),
                       vector, "VE")
            inc(vector.tensor_scalar(out=s.t2[0:1, 12:14], in0=s.t2[0:1, 10:12],
                                     scalar1=0.5, scalar2=None, op0=ALU.mult), "VE")
        vev3 = cnt["VE"]
        @block.tensor
        def _(tensor):
            w(tensor, "VE", vev3)
            inc(tensor.matmul(psw[:, 0:4], s.onesf1[:, :], s.t2[0:1, 10:14],
                              start=True, stop=True), "TE")
        tev2 = cnt["TE"]
        @block.vector
        def _(vector):
            w(vector, "TE", tev2)
            vector.tensor_scalar(out=s.wneg[:, 0:2], in0=psw[:, 0:2], scalar1=-1.0,
                                 scalar2=None, op0=ALU.mult)
            vector.tensor_copy(out=s.wtau[:, 0:1], in_=psw[:, 2:3])
            vector.tensor_scalar(out=s.wtau[:, 1:2], in0=psw[:, 2:3], scalar1=-1.0,
                                 scalar2=None, op0=ALU.mult)
            vector.tensor_copy(out=s.wtau[:, 2:3], in_=psw[:, 3:4])
            inc(vector.tensor_scalar(out=s.wtau[:, 3:4], in0=psw[:, 3:4], scalar1=-1.0,
                                     scalar2=None, op0=ALU.mult), "VE")
        vtau = cnt["VE"]
        sd_snaps = []
        ld2 = []
        q0v = cnt["VE"]

        def emit_load2(k):
            wi, c = k // 32, k % 32
            @block.gpsimd
            def _(g, wi=wi, c=c, k=k):
                buf = wld[:, (k % 4) * 1024:(k % 4 + 1) * 1024]
                if k >= 4:
                    w(g, "VE", q0v + k - 3)
                dinc(g.dma_start(out=buf, in_=self._w_piece_src(wi, c)), "GD")
            ld2.append(self.dsnap("GD"))

        for k in range(4):
            emit_load2(k)
        for k in range(64):
            wi, c = k // 32, k % 32
            tc = wi * 2
            qo = wq_o[:, (k % 2) * 1024:(k % 2 + 1) * 1024]
            @block.vector
            def _(vector, wi=wi, c=c, k=k, tc=tc, qo=qo):
                buf = wld[:, (k % 4) * 1024:(k % 4 + 1) * 1024]
                self.dwait(vector, "GD", ld2[k])
                if k == 0:
                    w(vector, "VE", vtau)
                if k >= 2:
                    self.dwait(vector, "SD", sd_snaps[k - 2])
                vector.tensor_single_scalar(out=wq_a[:, :], in_=buf,
                                            scalar=s.wtau[:, tc:tc + 1], op=ALU.is_gt)
                inc(vector.scalar_tensor_tensor(out=qo, in0=buf,
                                                scalar=s.wtau[:, tc + 1:tc + 2],
                                                in1=wq_a[:, :],
                                                op0=ALU.is_lt, op1=ALU.subtract), "VE")
            assert cnt["VE"] == q0v + k + 1
            veq = cnt["VE"]
            @block.gpsimd
            def _(g, wi=wi, c=c, veq=veq, qo=qo):
                w(g, "VE", veq)
                if wi == 0:
                    dc, cp = c // 4, c % 4
                    for h8 in range(8):
                        hc = cp * 8 + h8
                        dinc(g.dma_start(
                            out=s.scr_w1[hc * DC + dc:hc * DC + dc + 1, :]
                            .rearrange("a (p f) -> (a p) f", p=128),
                            in_=qo[:, h8 * 128:(h8 + 1) * 128]), "SD")
                else:
                    dinc(g.dma_start(out=s.scr_w2[c * 128:(c + 1) * 128, :],
                                     in_=qo[:, :]), "SD")
            sd_snaps.append(self.dsnap("SD"))
            if k + 4 < 64:
                emit_load2(k + 4)
        stw2.close()
        self._stw.close()

    def ph_a(self):
        s, nc, block = self, self.nc, self.block
        cnt, w, inc, dinc = self.cnt, self.w, self.inc, self.dinc
        sta = contextlib.ExitStack()
        xt = self.sb("xt", [128, 2 * D], F32, stack=sta)
        ln0b = self.sb("ln0b", [128, 2 * D], BF16, stack=sta)
        pst = self.psum("pst", [128, 2 * 1024], BF16, stack=sta)
        ln0T_v = s.ln0T.rearrange("p (dc f) -> p dc f", f=T)
        evac_done = []
        for tt in range(NT):
            p = tt % 2
            @block.sync
            def _(sync, tt=tt, p=p):
                if tt >= 2:
                    w(sync, "VE", evac_done[tt - 2][0])
                dinc(sync.dma_start(out=xt[:, p * D:(p + 1) * D],
                                   in_=s.xb[tt * 128:(tt + 1) * 128, :]), "LD")
            ldv = self.dsnap("LD")
            @block.vector
            def _(vector, p=p, ldv=ldv):
                self.dwait(vector, "LD", ldv)
                inc(self.stats_v(vector, xt[:, p * D:p * D + 512],
                                 xt[:, p * D + 512:(p + 1) * D], 0), "VE")
            ve1 = cnt["VE"]
            @block.scalar
            def _(scalar, ve1=ve1):
                w(scalar, "VE", ve1)
                inc(self.stats_s(scalar, 0), "SE")
            se1 = cnt["SE"]
            @block.vector
            def _(vector, p=p, se1=se1):
                w(vector, "SE", se1)
                inc(vector.tensor_scalar(out=ln0b[:, p * D:(p + 1) * D],
                                         in0=xt[:, p * D:(p + 1) * D],
                                         scalar1=s.sA[:, 0:1], scalar2=s.sA[:, 1:2],
                                         op0=ALU.subtract, op1=ALU.mult), "VE")
            ve2 = cnt["VE"]
            @block.tensor
            def _(tensor, tt=tt, p=p, ve2=ve2):
                w(tensor, "VE", ve2)
                if tt >= 2:
                    w(tensor, "VE", evac_done[tt - 2][1])
                for dc in range(DC):
                    i = tensor.transpose(pst[:, p * 1024 + dc * 128:p * 1024 + (dc + 1) * 128],
                                         ln0b[:, p * D + dc * 128:p * D + (dc + 1) * 128],
                                         s.idb[:, :])
                inc(i, "TE")
            te1 = cnt["TE"]
            @block.vector
            def _(vector, tt=tt, p=p, te1=te1):
                w(vector, "TE", te1)
                inc(vector.tensor_copy(out=ln0T_v[:, :, tt * 128:(tt + 1) * 128],
                                       in_=pst[:, p * 1024:(p + 1) * 1024]
                                       .rearrange("p (dc f) -> p dc f", f=128)), "VE")
            evac_done.append((ve2, cnt["VE"]))
        sta.close()

    def ph_b(self):
        s, nc, block = self, self.nc, self.block
        cnt, w, inc, dinc = self.cnt, self.w, self.inc, self.dinc
        stb = contextlib.ExitStack()
        wbA = self.sb("wbA", [128, 2 * 1024], BF16, stack=stb)
        wbB = self.sb("wbB", [128, 2 * 1024], BF16, stack=stb)
        itb = self.sb("itb", [128, T], BF16, stack=stb)
        atf = self.sb("atf", [128, T], F32, stack=stb)
        asq = self.sb("asq", [128, T], F32, stack=stb)
        gxp = self.sb("gxp", [128, T], BF16, stack=stb)
        ysc = self.sb("ysc", [128, T], BF16, stack=stb)
        psI = self.psum("psI", [128, 1024], F32, stack=stb)
        psR = self.psum("psR", [128, 1024], F32, stack=stb)
        psT = self.psum("psT", [128, 2 * 1024], BF16, stack=stb)
        ysb_v = s.ysb.rearrange("p (k f) -> p k f", f=D)

        cast_done = []
        scan_done = []
        g1_done = []
        tp_done = []
        tp_te = []

        def emit_tp(dcq):
            for g8 in range(4):
                bk = g8 % 2
                @block.tensor
                def _(tensor, dcq=dcq, g8=g8, bk=bk):
                    w(tensor, "VE", scan_done[dcq])
                    if g8 >= 2 or (dcq >= 1 and len(tp_done) > 0):
                        w(tensor, "VE")
                    for k in range(8):
                        tti = g8 * 8 + k
                        i = tensor.transpose(psT[:, bk * 1024 + k * 128:bk * 1024 + (k + 1) * 128],
                                             ysc[:, tti * 128:(tti + 1) * 128],
                                             s.idb[:, :])
                    inc(i, "TE")
                te_t = cnt["TE"]
                @block.vector
                def _(vector, dcq=dcq, g8=g8, bk=bk, te_t=te_t):
                    w(vector, "TE", te_t)
                    inc(vector.tensor_copy(
                        out=ysb_v[:, g8 * 8:(g8 + 1) * 8, dcq * 128:(dcq + 1) * 128],
                        in_=psT[:, bk * 1024:(bk + 1) * 1024]
                        .rearrange("p (k f) -> p k f", f=128)), "VE")
            tp_done.append(cnt["VE"])
            tp_te.append(cnt["TE"])

        @block.sync
        def _(sync):
            self.gate(sync)
        @block.gpsimd
        def _(g):
            self.gate(g, dma=False)
        se0b = cnt["SE"]

        def sigc(dc, proj, tb):
            return se0b + dc * 18 + proj * 8 + tb + 1

        for dc in range(DC):
            p = dc % 2
            @block.gpsimd
            def _(g, dc=dc, p=p):
                if dc >= 2:
                    w(g, "SE", sigc(dc - 2, 1, 7))
                for kc in range(DC):
                    dinc(g.dma_start(out=wbA[:, p * 1024 + kc * 128:p * 1024 + (kc + 1) * 128],
                                     in_=s.WiT[kc * 128:(kc + 1) * 128, dc * 128:(dc + 1) * 128]),
                         "GD")
                    dinc(g.dma_start(out=wbB[:, p * 1024 + kc * 128:p * 1024 + (kc + 1) * 128],
                                     in_=s.WrT[kc * 128:(kc + 1) * 128, dc * 128:(dc + 1) * 128]),
                         "GD")
            cast_done.append(self.dsnap("GD"))

            te_mm_l = {}
            for proj in range(2):
                ps_ = psI if proj == 0 else psR
                wb_ = wbA if proj == 0 else wbB
                for tb in range(DC):
                    bk = tb % 2
                    @block.tensor
                    def _(tensor, dc=dc, p=p, proj=proj, tb=tb, bk=bk, ps_=ps_, wb_=wb_):
                        if tb == 0 and proj == 0:
                            self.dwait(tensor, "GD", cast_done[dc])
                        if tb >= 2:
                            w(tensor, "SE", sigc(dc, proj, tb - 2))
                        elif dc > 0:
                            w(tensor, "SE", sigc(dc - 1, proj, tb + 6))
                        for kc in range(DC):
                            i = tensor.matmul(
                                ps_[:, bk * 512:(bk + 1) * 512],
                                wb_[:, p * 1024 + kc * 128:p * 1024 + (kc + 1) * 128],
                                s.ln0T[:, kc * T + tb * 512:kc * T + (tb + 1) * 512],
                                start=(kc == 0), stop=(kc == DC - 1))
                        inc(i, "TE")
                    te_mm_l[(proj, tb)] = cnt["TE"]
            @block.vector
            def _(vector, dc=dc):
                w(vector, "SE", se0b + dc * 18 + 17)
                inc(vector.tensor_tensor(out=asq[:, :], in0=atf[:, :], in1=atf[:, :],
                                         op=ALU.mult), "VE")
            ve_sq = cnt["VE"]
            @block.scalar
            def _(scalar, dc=dc, te_mm_l=dict(te_mm_l), ve_sq=ve_sq):
                for proj in range(2):
                    ps_ = psI if proj == 0 else psR
                    bias_ = s.bi_s if proj == 0 else s.br_s
                    dst = itb if proj == 0 else atf
                    for tb in range(DC):
                        bk = tb % 2
                        w(scalar, "TE", te_mm_l[(proj, tb)])
                        if dc >= 1 and tb == 0 and proj == 0:
                            w(scalar, "VE", g1_done[dc - 1])
                        inc(scalar.activation(out=dst[:, tb * 512:(tb + 1) * 512],
                                              in_=ps_[:, bk * 512:(bk + 1) * 512],
                                              func=AF.Sigmoid, bias=bias_[:, dc:dc + 1]),
                            "SE")
                inc(scalar.activation(out=atf[:, :], in_=atf[:, :], func=AF.Exp,
                                      scale=s.c8v[:, 0:1]), "SE")
                w(scalar, "VE", ve_sq)
                inc(scalar.activation(out=asq[:, :], in_=asq[:, :], func=AF.Sqrt,
                                      scale=-1.0, bias=1.0), "SE")
            assert cnt["SE"] == se0b + dc * 18 + 18
            se_sqrt = cnt["SE"]
            if dc >= 1:
                emit_tp(dc - 1)
            @block.vector
            def _(vector, dc=dc, se_sqrt=se_sqrt):
                w(vector, "SE", se_sqrt)
                if dc >= 1:
                    w(vector, "TE", tp_te[dc - 1])
                vector.tensor_tensor(out=gxp[:, :], in0=itb[:, :],
                                     in1=s.ln0T[:, dc * T:(dc + 1) * T], op=ALU.mult)
                vector.tensor_tensor(out=gxp[:, :], in0=gxp[:, :], in1=asq[:, :], op=ALU.mult)
                inc(vector.tensor_tensor_scan(out=ysc[:, :], data0=atf[:, :],
                                              data1=gxp[:, :], initial=0.0,
                                              op0=ALU.mult, op1=ALU.add), "VE")
            scan_done.append(cnt["VE"])
            g1_done.append(cnt["VE"])
        emit_tp(DC - 1)
        stb.close()

    def ph_c1(self):
        s, nc, block = self, self.nc, self.block
        cnt, w, inc, dinc = self.cnt, self.w, self.inc, self.dinc
        stc = contextlib.ExitStack()
        xt = self.sb("xtc", [128, 2 * D], F32, stack=stc)
        ysf = self.sb("ysfc", [128, D], F32, stack=stc)
        x1t = self.sb("x1t", [128, D], F32, stack=stc)
        x2t = self.sb("x2t", [128, 2 * D], F32, stack=stc)
        dl = self.sb("dl", [128, D], F32, stack=stc)
        x1bb = self.sb("x1bb", [128, D], BF16, stack=stc)
        x1Tt = self.sb("x1Tt", [128, D], BF16, stack=stc)
        hhf = self.sb("hhf", [128, 1024], F32, stack=stc)
        hh3 = self.sb("hh3", [128, 1024], BF16, stack=stc)
        sff = self.sb("sff", [128, 256], F32, stack=stc)
        g1b = self.sb("g1b", [128, 64], BF16, stack=stc)
        gTt = self.sb("gTt", [64, 128], BF16, stack=stc)
        psB = self.psum("psB", [128, 1024], BF16, stack=stc)
        psS = self.psum("psS", [128, 512], F32, stack=stc)
        psG = self.psum("psG", [64, 128], BF16, stack=stc)
        psO = self.psum("psO", [128, 1024], F32, stack=stc)

        st_done = []
        @block.sync
        def _(sync):
            self.gate(sync)
        for tt in range(NT):
            p = tt % 2
            @block.sync
            def _(sync, tt=tt, p=p):
                if tt >= 2:
                    w(sync, "VE", st_done[tt - 2][0])
                dinc(sync.dma_start(out=xt[:, p * D:(p + 1) * D],
                                   in_=s.xb[tt * 128:(tt + 1) * 128, :]), "LD")
            ldv = self.dsnap("LD")
            @block.vector
            def _(vector, tt=tt):
                if tt >= 2:
                    self.dwait(vector, "SD", st_done[tt - 2][1])
                vector.tensor_copy(out=ysf[:, :], in_=s.ysb[:, tt * D:(tt + 1) * D])
                inc(self.stats_v(vector, ysf[:, 0:512], ysf[:, 512:1024], 0), "VE")
            ve1 = cnt["VE"]
            @block.scalar
            def _(scalar, ve1=ve1):
                w(scalar, "VE", ve1)
                inc(self.stats_s(scalar, 0), "SE")
            se1 = cnt["SE"]
            @block.vector
            def _(vector, p=p, ldv=ldv, se1=se1):
                w(vector, "SE", se1)
                self.dwait(vector, "LD", ldv)
                vector.tensor_scalar(out=dl[:, :], in0=ysf[:, :], scalar1=s.sA[:, 0:1],
                                     scalar2=s.sA[:, 1:2], op0=ALU.subtract, op1=ALU.mult)
                vector.tensor_tensor(out=x1t[:, :], in0=xt[:, p * D:(p + 1) * D],
                                     in1=dl[:, :], op=ALU.add)
                inc(self.stats_v(vector, x1t[:, 0:512], x1t[:, 512:1024], 2), "VE")
            ve2 = cnt["VE"]
            @block.scalar
            def _(scalar, ve2=ve2):
                w(scalar, "VE", ve2)
                inc(self.stats_s(scalar, 2), "SE")
            se2 = cnt["SE"]
            @block.vector
            def _(vector, se2=se2):
                w(vector, "SE", se2)
                self.chain(vector.tensor_scalar(out=s.sA[:, 4:5], in0=s.sA[:, 2:3],
                                                scalar1=s.sA[:, 3:4], scalar2=-1.0,
                                                op0=ALU.mult, op1=ALU.mult), vector, "VE")
                inc(vector.tensor_copy(out=x1bb[:, :], in_=x1t[:, :]), "VE")
            ve3 = cnt["VE"]
            @block.tensor
            def _(tensor, tt=tt, ve3=ve3):
                w(tensor, "VE", ve3)
                if tt >= 1:
                    w(tensor, "VE")
                for dc in range(DC):
                    i = tensor.transpose(psB[:, dc * 128:(dc + 1) * 128],
                                         x1bb[:, dc * 128:(dc + 1) * 128], s.idb[:, :])
                inc(i, "TE")
            te1 = cnt["TE"]
            @block.vector
            def _(vector, te1=te1):
                w(vector, "TE", te1)
                inc(vector.tensor_copy(out=x1Tt[:, :], in_=psB[:, :]), "VE")
            ve4 = cnt["VE"]
            @block.tensor
            def _(tensor, tt=tt, ve4=ve4):
                w(tensor, "VE", ve4)
                for dc in range(DC):
                    i = tensor.matmul(psS[:, 0:64], x1Tt[:, dc * 128:(dc + 1) * 128],
                                      s.winb[:, dc * 64:(dc + 1) * 64],
                                      start=(dc == 0), stop=(dc == DC - 1))
                inc(i, "TE")
            te2 = cnt["TE"]
            @block.vector
            def _(vector, te2=te2):
                w(vector, "TE", te2)
                self.chain(vector.tensor_scalar(out=sff[:, 0:64], in0=psS[:, 0:64],
                                                scalar1=s.sA[:, 3:4], scalar2=None,
                                                op0=ALU.mult), vector, "VE")
                self.chain(vector.scalar_tensor_tensor(out=sff[:, 64:128], in0=s.cswb[:, :],
                                                       scalar=s.sA[:, 4:5], in1=sff[:, 0:64],
                                                       op0=ALU.mult, op1=ALU.add), vector, "VE")
                inc(vector.tensor_tensor(out=sff[:, 128:192], in0=sff[:, 64:128],
                                         in1=s.binb[:, :], op=ALU.add), "VE")
            ve5 = cnt["VE"]
            @block.gpsimd
            def _(g, ve5=ve5, tt=tt):
                if tt == 0:
                    self.dwait(g, "GD", self._gd_ph0)
                w(g, "VE", ve5)
                s_b = sff[:, 128:192].rearrange("p (m o) -> p m o", o=1).broadcast_to((128, 64, 16))
                g.tensor_tensor(out=hhf[:, :].rearrange("p (m i) -> p m i", i=16),
                                in0=s_b, in1=s.W1b[:, :].rearrange("p (m i) -> p m i", i=16),
                                op=ALU.mult)
                inc(g.tensor_tensor(out=hhf[:, :], in0=hhf[:, :], in1=s.b1b[:, :],
                                    op=ALU.add), "GE")
            ge1 = cnt["GE"]
            @block.vector
            def _(vector, ge1=ge1):
                w(vector, "GE", ge1)
                inc(vector.tensor_scalar(out=hh3[:, :], in0=hhf[:, :], scalar1=0.0,
                                         scalar2=None, op0=ALU.max), "VE")
            ve3r = cnt["VE"]
            @block.gpsimd
            def _(g, ve3r=ve3r):
                w(g, "VE", ve3r)
                inc(g.tensor_tensor(out=hhf[:, :], in0=hh3[:, :], in1=s.W2b[:, :],
                                    op=ALU.mult), "GE")
            ge2 = cnt["GE"]
            @block.vector
            def _(vector, ge2=ge2):
                w(vector, "GE", ge2)
                self.chain(vector.reduce_sum(out=sff[:, 192:256].rearrange("p (m o) -> p m o", o=1),
                                             in_=hhf[:, :].rearrange("p (m i) -> p m i", i=16),
                                             axis=AX.X), vector, "VE")
                inc(vector.tensor_tensor(out=g1b[:, :], in0=sff[:, 192:256], in1=s.b2b[:, :],
                                         op=ALU.add), "VE")
            ve6 = cnt["VE"]
            @block.tensor
            def _(tensor, ve6=ve6, tt=tt):
                w(tensor, "VE", ve6)
                inc(tensor.transpose(psG[0:64, :], g1b[:, :], s.idb[:, :]), "TE")
            te3 = cnt["TE"]
            @block.vector
            def _(vector, te3=te3):
                w(vector, "TE", te3)
                inc(vector.tensor_copy(out=gTt[:, :], in_=psG[0:64, :]), "VE")
            ve7 = cnt["VE"]
            @block.tensor
            def _(tensor, ve7=ve7, tt=tt):
                w(tensor, "VE", ve7)
                if tt >= 1:
                    w(tensor, "VE")
                tensor.matmul(psO[:, 0:512], gTt[:, :], s.woutb[:, 0:512], start=True, stop=True)
                inc(tensor.matmul(psO[:, 512:1024], gTt[:, :], s.woutb[:, 512:1024],
                                  start=True, stop=True), "TE")
            te4 = cnt["TE"]
            @block.vector
            def _(vector, te4=te4):
                w(vector, "TE", te4)
                vector.tensor_tensor(out=hhf[:, :], in0=psO[:, :], in1=s.boutb[:, :], op=ALU.add)
                inc(self.stats_v(vector, hhf[:, 0:512], hhf[:, 512:1024], 6), "VE")
            ve8 = cnt["VE"]
            @block.scalar
            def _(scalar, ve8=ve8):
                w(scalar, "VE", ve8)
                inc(self.stats_s(scalar, 6), "SE")
            se4 = cnt["SE"]
            @block.vector
            def _(vector, p=p, se4=se4):
                w(vector, "SE", se4)
                vector.tensor_scalar(out=dl[:, :], in0=hhf[:, :], scalar1=s.sA[:, 6:7],
                                     scalar2=s.sA[:, 7:8], op0=ALU.subtract, op1=ALU.mult)
                inc(vector.tensor_tensor(out=x2t[:, p * D:(p + 1) * D], in0=x1t[:, :],
                                         in1=dl[:, :], op=ALU.add), "VE")
            ve9 = cnt["VE"]
            @block.gpsimd
            def _(g, tt=tt, p=p, ve9=ve9):
                w(g, "VE", ve9)
                dinc(g.dma_start(out=s.scr_x2[tt * 128:(tt + 1) * 128, :],
                                in_=x2t[:, p * D:(p + 1) * D]), "SD")
            st_done.append((ve9, self.dsnap("SD")))
        stc.close()

    def ph_mn(self, do_m=True, do_n=True):
        s, nc, block = self, self.nc, self.block
        cnt, w, inc, dinc = self.cnt, self.w, self.inc, self.dinc
        stm = contextlib.ExitStack()
        sfx = "m" if do_m else "n"
        CW = D + LAT + LAT + D
        qld = self.sb("qld" + sfx, [128, 2 * CW], F32, stack=stm)
        qldb = self.sb("qldb" + sfx, [128, 2 * CW], BF16, stack=stm)
        psM = self.psum("psM" + sfx, [128, 512], F32, stack=stm)
        psN = self.psum("psN" + sfx, [128, 1024], F32, stack=stm)
        Mb_v = s.Mbuf.rearrange("p (dc f) -> p dc f", f=512) if do_m else None
        @block.sync
        def _(sync):
            self.gate(sync)
        for h in range(H):
            @block.sync
            def _(sync, h=h):
                if h >= 1:
                    w(sync, "VE")
                for oc2 in range(2):
                    oc = h * 2 + oc2
                    b0 = oc2 * CW
                    dinc(sync.dma_start(out=qld[:, b0:b0 + D],
                                       in_=s.Wq[oc * 128:(oc + 1) * 128, :]), "LD")
                    dinc(sync.dma_start(out=qld[:, b0 + D:b0 + D + LAT],
                                       in_=s.Wk_up[oc * 128:(oc + 1) * 128, :]), "LD")
                    dinc(sync.dma_start(out=qld[:, b0 + D + LAT:b0 + D + 2 * LAT],
                                       in_=s.Wv_up[oc * 128:(oc + 1) * 128, :]), "LD")
                    dinc(sync.dma_start(out=qld[:, b0 + D + 2 * LAT:b0 + CW],
                                       in_=s.WoT[oc * 128:(oc + 1) * 128, :]), "LD")
            ldm = self.dsnap("LD")
            @block.vector
            def _(vector, ldm=ldm):
                self.dwait(vector, "LD", ldm)
                inc(vector.tensor_copy(out=qldb[:, :], in_=qld[:, :]), "VE")
            vem = cnt["VE"]
            @block.tensor
            def _(tensor, h=h, vem=vem):
                w(tensor, "VE", vem)
                if h >= 1:
                    w(tensor, "VE")
                if do_m:
                    for dc in range(DC):
                        for oc2 in range(2):
                            b0 = oc2 * CW
                            i = tensor.matmul(psM[:, dc * 64:(dc + 1) * 64],
                                              qldb[:, b0 + dc * 128:b0 + (dc + 1) * 128],
                                              qldb[:, b0 + D:b0 + D + LAT],
                                              start=(oc2 == 0), stop=(oc2 == 1))
                if do_n:
                    for nb2 in range(2):
                        for oc2 in range(2):
                            b0 = oc2 * CW
                            i = tensor.matmul(psN[0:64, nb2 * 512:(nb2 + 1) * 512],
                                              qldb[:, b0 + D + LAT:b0 + D + 2 * LAT],
                                              qldb[:, b0 + D + 2 * LAT + nb2 * 512:
                                                   b0 + D + 2 * LAT + (nb2 + 1) * 512],
                                              start=(oc2 == 0), stop=(oc2 == 1))
                inc(i, "TE")
            tem = cnt["TE"]
            @block.vector
            def _(vector, h=h, tem=tem):
                w(vector, "TE", tem)
                if do_m:
                    i = vector.tensor_copy(out=Mb_v[:, :, h * 64:(h + 1) * 64],
                                           in_=psM[:, :].rearrange("p (dc f) -> p dc f", f=64))
                if do_n:
                    c = h // 2
                    r = 64 * (h % 2)
                    i = vector.tensor_copy(out=s.Nbuf[r:r + 64, c * D:(c + 1) * D],
                                           in_=psN[0:64, :])
                inc(i, "VE")
        stm.close()

    def ph_c2(self):
        s, nc, block = self, self.nc, self.block
        cnt, w, inc, dinc = self.cnt, self.w, self.inc, self.dinc
        stc = contextlib.ExitStack()
        x2l = self.sb("x2l", [128, 2 * D], F32, stack=stc)
        ln2b = self.sb("ln2b", [128, 2 * D], BF16, stack=stc)
        pst = self.psum("pst2", [128, 2 * 1024], BF16, stack=stc)
        ln2T_v = s.ln2T.rearrange("p (dc f) -> p dc f", f=T)
        done = []
        for tt in range(NT):
            p = tt % 2
            @block.sync
            def _(sync, tt=tt, p=p):
                if tt >= 2:
                    w(sync, "VE", done[tt - 2][0])
                dinc(sync.dma_start(out=x2l[:, p * D:(p + 1) * D],
                                   in_=s.scr_x2[tt * 128:(tt + 1) * 128, :]), "LD")
            ldv = self.dsnap("LD")
            @block.vector
            def _(vector, p=p, ldv=ldv):
                self.dwait(vector, "LD", ldv)
                inc(self.stats_v(vector, x2l[:, p * D:p * D + 512],
                                 x2l[:, p * D + 512:(p + 1) * D], 0), "VE")
            ve1 = cnt["VE"]
            @block.scalar
            def _(scalar, ve1=ve1):
                w(scalar, "VE", ve1)
                inc(self.stats_s(scalar, 0), "SE")
            se1 = cnt["SE"]
            @block.vector
            def _(vector, tt=tt, p=p, se1=se1):
                w(vector, "SE", se1)
                i = vector.tensor_scalar(out=ln2b[:, p * D:(p + 1) * D],
                                         in0=x2l[:, p * D:(p + 1) * D],
                                         scalar1=s.sA[:, 0:1], scalar2=s.sA[:, 1:2],
                                         op0=ALU.subtract, op1=ALU.mult)
                if tt <= ENT - 1:
                    i = vector.tensor_single_scalar(out=s.x2win[:, tt * D:(tt + 1) * D],
                                                    in_=x2l[:, p * D:(p + 1) * D],
                                                    scalar=s.jv[:, 1:2], op=ALU.mult)
                if tt >= NT - ENT:
                    v = tt - (NT - ENT)
                    i = vector.scalar_tensor_tensor(out=s.x2win[:, v * D:(v + 1) * D],
                                                    in0=x2l[:, p * D:(p + 1) * D],
                                                    scalar=s.jv[:, 0:1],
                                                    in1=s.x2win[:, v * D:(v + 1) * D],
                                                    op0=ALU.mult, op1=ALU.add)
                inc(i, "VE")
            ve2 = cnt["VE"]
            @block.tensor
            def _(tensor, tt=tt, p=p, ve2=ve2):
                w(tensor, "VE", ve2)
                if tt >= 2:
                    w(tensor, "VE", done[tt - 2][1])
                for dc in range(DC):
                    i = tensor.transpose(pst[:, p * 1024 + dc * 128:p * 1024 + (dc + 1) * 128],
                                         ln2b[:, p * D + dc * 128:p * D + (dc + 1) * 128],
                                         s.idb[:, :])
                inc(i, "TE")
            te1 = cnt["TE"]
            @block.vector
            def _(vector, tt=tt, p=p, te1=te1):
                w(vector, "TE", te1)
                inc(vector.tensor_copy(out=ln2T_v[:, :, tt * 128:(tt + 1) * 128],
                                       in_=pst[:, p * 1024:(p + 1) * 1024]
                                       .rearrange("p (dc f) -> p dc f", f=128)), "VE")
            done.append((ve2, cnt["VE"]))
        stc.close()

    def ph_d(self):
        s, nc, block = self, self.nc, self.block
        cnt, w, inc, dinc = self.cnt, self.w, self.inc, self.dinc
        std = contextlib.ExitStack()
        psL = self.psum("psL", [128, 1024], F32, stack=std)
        psQ = self.psum("psQ", [128, 1024], F32, stack=std)
        ve_ln2T = cnt["VE"]
        @block.tensor
        def _(tensor):
            self.gate(tensor, dma=False)
        for tb in range(DC):
            bk = tb % 2
            @block.tensor
            def _(tensor, tb=tb, bk=bk):
                w(tensor, "VE", ve_ln2T)
                if tb >= 2:
                    w(tensor, "VE")
                for kc in range(DC):
                    i = tensor.matmul(psL[:, bk * 512:(bk + 1) * 512],
                                      s.kvdb[:, kc * 128:(kc + 1) * 128],
                                      s.ln2T[:, kc * T + tb * 512:kc * T + (tb + 1) * 512],
                                      start=(kc == 0), stop=(kc == DC - 1))
                inc(i, "TE")
            teL = cnt["TE"]
            @block.vector
            def _(vector, tb=tb, bk=bk, teL=teL):
                w(vector, "TE", teL)
                inc(vector.tensor_copy(out=s.latT[:, tb * 512:(tb + 1) * 512],
                                       in_=psL[:, bk * 512:(bk + 1) * 512]), "VE")
        halves = [(0, 1024, [(0, 512), (512, 512)]),
                  (1024, 1152, [(0, 512), (512, 512), (1024, 128)])]
        for hw0, hwn, qcs in halves:
            @block.vector
            def _(vector, hw0=hw0, hwn=hwn):
                w(vector, "VE", ve_ln2T)
                w(vector, "TE")
                for dcc in range(DC):
                    vector.tensor_single_scalar(out=s.ln2Twin[:, dcc * 1152:dcc * 1152 + hwn],
                                                in_=s.ln2T[:, dcc * T + hw0:dcc * T + hw0 + hwn],
                                                scalar=s.jv[:, 1:2], op=ALU.mult)
                    i = vector.scalar_tensor_tensor(
                        out=s.ln2Twin[:, dcc * 1152:dcc * 1152 + hwn],
                        in0=s.ln2T[:, dcc * T + 1920 + hw0:dcc * T + 1920 + hw0 + hwn],
                        scalar=s.jv[:, 0:1],
                        in1=s.ln2Twin[:, dcc * 1152:dcc * 1152 + hwn],
                        op0=ALU.mult, op1=ALU.add)
                inc(i, "VE")
            ve_win = cnt["VE"]
            for h in range(H):
                for qi, (q0, qn) in enumerate(qcs):
                    bk = (h * len(qcs) + qi) % 2
                    @block.tensor
                    def _(tensor, h=h, qi=qi, q0=q0, qn=qn, bk=bk, hw0=hw0, ve_win=ve_win):
                        if h == 0 and qi == 0:
                            w(tensor, "VE", ve_win)
                        if h * len(qcs) + qi >= 2:
                            w(tensor, "VE")
                        r = 64 * (h % 2)
                        for dc in range(DC):
                            i = tensor.matmul(psQ[r:r + 64, bk * 512:bk * 512 + qn],
                                              s.Mbuf[:, dc * 512 + h * 64:dc * 512 + (h + 1) * 64],
                                              s.ln2Twin[:, dc * 1152 + q0:dc * 1152 + q0 + qn],
                                              start=(dc == 0), stop=(dc == DC - 1),
                                              tile_position=(0, r))
                        inc(i, "TE")
                    teQ = cnt["TE"]
                    @block.vector
                    def _(vector, h=h, q0=q0, qn=qn, bk=bk, teQ=teQ, hw0=hw0):
                        w(vector, "TE", teQ)
                        r = 64 * (h % 2)
                        qg = hw0 + q0
                        inc(vector.tensor_copy(
                            out=s.qpT[r:r + 64, (h // 2) * ETK + qg:(h // 2) * ETK + qg + qn],
                            in_=psQ[r:r + 64, bk * 512:bk * 512 + qn]), "VE")
        std.close()

    def ph_e(self, st_e):
        s, nc, block = self, self.nc, self.block
        cnt, w, inc, dinc = self.cnt, self.w, self.inc, self.dinc
        latkm = self.sb("latkm", [128, NT * 65], BF16, stack=st_e)
        ETb = self.sb("ETb", [128, 2 * 1536], BF16, stack=st_e)
        oTh = self.sb("oTh", [128, 512], F32, stack=st_e)
        ones2d = self.sb("ones2d", [128, 64], F32, stack=st_e)
        drow = self.sb("drow", [1, 512], F32, stack=st_e)
        se_prior = cnt["SE"]
        @block.vector
        def _(vector):
            self.gate(vector, dma=False)
            vector.memset(ones2d[:, :], 1.0)
            inc(vector.memset(latkm[:, :], 1.0), "VE")
        veo = cnt["VE"]
        with nc.psum_tensor("psTkm", [128, 1024], BF16) as psTkm:
            for kb in range(NT):
                bk = kb % 2
                @block.tensor
                def _(tensor, kb=kb, bk=bk, veo=veo):
                    if kb == 0:
                        w(tensor, "VE", veo)
                    if kb >= 2:
                        w(tensor, "SE")
                    inc(tensor.transpose(psTkm[:, bk * 512:bk * 512 + 64],
                                         s.latT[0:64, kb * 128:(kb + 1) * 128],
                                         s.idb[0:64, 0:64]), "TE")
                tek = cnt["TE"]
                @block.scalar
                def _(scalar, kb=kb, bk=bk, tek=tek):
                    w(scalar, "TE", tek)
                    inc(scalar.copy(out=latkm[:, kb * 65:kb * 65 + 64],
                                    in_=psTkm[:, bk * 512:bk * 512 + 64]), "SE")
        se_lat = cnt["SE"]
        psS1 = self.psum("psS1", [128, 1536], F32, stack=st_e)
        psS2 = self.psum("psS2", [128, 1536], F32, stack=st_e)
        psOv = self.psum("psOv", [128, 512], F32, stack=st_e)
        psDb = self.psum("psDb", [128, 512], F32, stack=st_e)

        groups = []
        k0 = 0
        while k0 < NT:
            gn = 2 if NT - k0 == 4 else min(3, NT - k0)
            groups.append((k0, gn))
            k0 += gn

        rDb = self.sb("rDb", [64, 512], F32, stack=st_e)
        first = True
        prev_tail = None
        NG = len(groups)

        def emit_tail_b(pt):
            ph, pq0, pqn, pteAV, pveC = pt
            @block.tensor
            def _(tensor, pqn=pqn, pveC=pveC):
                w(tensor, "VE", pveC)
                inc(tensor.matmul(psDb[0:64, 0:pqn], ones2d[64:65, 0:64],
                                  oTh[64:65, 0:pqn], start=True, stop=True), "TE")
            teD = cnt["TE"]
            @block.vector
            def _(vector, ph=ph, pq0=pq0, pqn=pqn, teD=teD):
                w(vector, "TE", teD)
                self.chain(vector.reciprocal(out=rDb[:, 0:pqn], in_=psDb[0:64, 0:pqn]),
                           vector, "VE")
                c = ph // 2
                r = 64 * (ph % 2)
                inc(vector.tensor_tensor(
                    out=s.oTn[r:r + 64, c * ETK + pq0:c * ETK + pq0 + pqn],
                    in0=oTh[0:64, 0:pqn], in1=rDb[:, 0:pqn], op=ALU.mult), "VE")

        for h in range(H):
            for qi, (q0, qn) in enumerate(QCH):
                sE0 = cnt["SE"]
                teS_l = [None] * NG

                def emit_scores(gi, h=h, q0=q0, qn=qn, first=first, sE0=sE0):
                    k0, gn = groups[gi]
                    psS = psS1 if gi % 2 == 0 else psS2
                    @block.tensor
                    def _(tensor, gi=gi, k0=k0, gn=gn, psS=psS):
                        if first and gi <= 1:
                            w(tensor, "SE", se_lat)
                        elif gi <= 1:
                            w(tensor, "SE", sE0)
                        else:
                            w(tensor, "SE", sE0 + gi - 1)
                        r = 64 * (h % 2)
                        for k in range(gn):
                            kb = k0 + k
                            i = tensor.matmul(psS[:, k * 512:k * 512 + qn],
                                              s.latT[r:r + 64, kb * 128:(kb + 1) * 128],
                                              s.qpT[r:r + 64, (h // 2) * ETK + q0:(h // 2) * ETK + q0 + qn],
                                              start=True, stop=True,
                                              tile_position=(r, 0))
                        inc(i, "TE")
                    teS_l[gi] = cnt["TE"]

                def emit_av(gi, qn=qn, sE0=sE0, pt=prev_tail):
                    k0, gn = groups[gi]
                    ping = gi % 2
                    @block.tensor
                    def _(tensor, gi=gi, k0=k0, gn=gn, ping=ping):
                        w(tensor, "SE", sE0 + gi + 1)
                        if gi == 0 and pt is not None:
                            w(tensor, "VE", pt[4])
                        for k in range(gn):
                            kb = k0 + k
                            i = tensor.matmul(
                                psOv[0:65, 0:qn],
                                latkm[:, kb * 65:(kb + 1) * 65],
                                ETb[:, ping * 1536 + k * 512:ping * 1536 + k * 512 + qn],
                                start=(kb == 0), stop=(kb == NT - 1))
                        inc(i, "TE")

                emit_scores(0)
                emit_scores(1)
                emit_av(0)
                if prev_tail is not None:
                    emit_tail_b(prev_tail)
                    prev_tail = None
                for gi in range(2, NG):
                    emit_scores(gi)
                    emit_av(gi - 1)
                emit_av(NG - 1)
                teAV = cnt["TE"]
                @block.scalar
                def _(scalar, qn=qn, teS_l=tuple(teS_l)):
                    for gi in range(NG):
                        k0, gn = groups[gi]
                        ping = gi % 2
                        psS = psS1 if ping == 0 else psS2
                        w(scalar, "TE", teS_l[gi])
                        inc(scalar.activation(
                            out=ETb[:, ping * 1536:ping * 1536 + gn * 512]
                            .rearrange("p (k f) -> p k f", f=512)[:, :, 0:qn],
                            in_=psS[:, 0:gn * 512]
                            .rearrange("p (k f) -> p k f", f=512)[:, :, 0:qn],
                            func=AF.Exp, scale=SCALE, bias=s.eshv[:, 0:1]), "SE")
                assert cnt["SE"] == sE0 + NG
                @block.vector
                def _(vector, qn=qn, teAV=teAV):
                    w(vector, "TE", teAV)
                    inc(vector.tensor_copy(out=oTh[0:65, 0:qn], in_=psOv[0:65, 0:qn]),
                        "VE")
                prev_tail = (h, q0, qn, teAV, cnt["VE"])
                first = False
        emit_tail_b(prev_tail)

    def ph_f(self, st_f):
        s, nc, block = self, self.nc, self.block
        cnt, w, inc, dinc = self.cnt, self.w, self.inc, self.dinc
        dl2 = self.sb("dl2", [128, D], F32, stack=st_f)
        psA = self.psum("psA", [128, 1024], F32, stack=st_f)
        psA2 = self.psum("psA2", [128, 1024], F32, stack=st_f)
        ve_oTn = cnt["VE"]
        @block.tensor
        def _(tensor):
            self.gate(tensor, dma=False)
        for v in range(ENT):
            pp = v % 2
            psX = psA if pp == 0 else psA2
            @block.tensor
            def _(tensor, v=v, psX=psX):
                if v == 0:
                    w(tensor, "VE", ve_oTn)
                if v >= 2:
                    w(tensor, "VE")
                for c in range(4):
                    for hf2 in range(2):
                        i = tensor.matmul(psX[:, hf2 * 512:(hf2 + 1) * 512],
                                          s.oTn[:, c * ETK + v * 128:c * ETK + (v + 1) * 128],
                                          s.Nbuf[:, c * D + hf2 * 512:c * D + (hf2 + 1) * 512],
                                          start=(c == 0), stop=(c == 3))
                inc(i, "TE")
            teA = cnt["TE"]
            @block.vector
            def _(vector, psX=psX, teA=teA):
                w(vector, "TE", teA)
                inc(self.stats_v(vector, psX[:, 0:512], psX[:, 512:1024], 0), "VE")
            ve1 = cnt["VE"]
            @block.scalar
            def _(scalar, ve1=ve1):
                w(scalar, "VE", ve1)
                inc(self.stats_s(scalar, 0), "SE")
            se1 = cnt["SE"]
            @block.vector
            def _(vector, v=v, psX=psX, se1=se1):
                w(vector, "SE", se1)
                vector.tensor_scalar(out=dl2[:, :], in0=psX[:, :], scalar1=s.sA[:, 0:1],
                                     scalar2=s.sA[:, 1:2], op0=ALU.subtract, op1=ALU.mult)
                inc(vector.tensor_tensor(out=s.x2win[:, v * D:(v + 1) * D],
                                         in0=s.x2win[:, v * D:(v + 1) * D], in1=dl2[:, :],
                                         op=ALU.add), "VE")

    def ph_g0(self, stg):
        s, nc, block = self, self.nc, self.block
        cnt, w, inc, dinc = self.cnt, self.w, self.inc, self.dinc
        st0 = contextlib.ExitStack()
        qib = self.sb("qib", [128, D], BF16, stack=st0)
        un = self.sb("un", [128, D], F32, stack=st0)
        ni = self.sb("ni", [128, D], I32, stack=st0)
        nf = self.sb("nf", [128, D], F32, stack=st0)
        gf = self.sb("gf", [128, D], F32, stack=st0)
        sxrow = self.sb("sxrow", [1, ETK], F32, stack=st0)
        psQt = self.psum("psQt", [128, 2 * 1024], BF16, stack=st0)
        psRw = self.psum("psRw", [128, 512], F32, stack=st0)
        qT_v = s.qT.rearrange("p (dc f) -> p dc f", f=ETK)
        @block.vector
        def _(vector):
            self.gate(vector)
        for v in range(ENT):
            pp = v % 2
            @block.vector
            def _(vector, v=v):
                inc(self.stats_v(vector, s.x2win[:, v * D:v * D + 512],
                                 s.x2win[:, v * D + 512:(v + 1) * D], 2), "VE")
            ve2 = cnt["VE"]
            @block.scalar
            def _(scalar, ve2=ve2):
                w(scalar, "VE", ve2)
                inc(self.stats_s(scalar, 2), "SE")
            se2 = cnt["SE"]
            @block.vector
            def _(vector, v=v, se2=se2):
                w(vector, "SE", se2)
                self.chain(vector.tensor_scalar(out=un[:, :], in0=s.x2win[:, v * D:(v + 1) * D],
                                                scalar1=s.sA[:, 2:3], scalar2=None,
                                                op0=ALU.subtract), vector, "VE")
                self.chain(vector.reduce_max(out=s.sA[:, 4:5], in_=un[:, :], axis=AX.X,
                                             apply_absolute_value=True), vector, "VE")
                self.chain(vector.tensor_scalar(out=s.sA[:, 4:5], in0=s.sA[:, 4:5],
                                                scalar1=s.sA[:, 3:4], scalar2=1e-5,
                                                op0=ALU.mult, op1=ALU.max), vector, "VE")
                vector.tensor_scalar(out=s.sxcols[:, v:v + 1], in0=s.sA[:, 4:5],
                                     scalar1=1.0 / 127.0, scalar2=None, op0=ALU.mult)
                self.chain(vector.reciprocal(out=s.sA[:, 5:6], in_=s.sA[:, 4:5]), vector, "VE")
                self.chain(vector.tensor_scalar(out=s.sA[:, 6:7], in0=s.sA[:, 5:6],
                                                scalar1=s.sA[:, 3:4], scalar2=127.0,
                                                op0=ALU.mult, op1=ALU.mult), vector, "VE")
                vector.tensor_scalar(out=un[:, :], in0=un[:, :], scalar1=s.sA[:, 6:7],
                                     scalar2=128.0, op0=ALU.mult, op1=ALU.add)
                inc(self.emit_round_ladder(vector, un[:, :], ni[:, :], nf[:, :], gf[:, :],
                                           qib[:, :], -128), "VE")
            ve3 = cnt["VE"]
            if self.debug == "phg0":
                @block.sync
                def _(sync, v=v, ve3=ve3):
                    w(sync, "VE", ve3)
                    self.dinc(sync.dma_start(out=self._dbg_un[v * 128:(v + 1) * 128, :],
                                             in_=un[:, :]), "ST")
            @block.tensor
            def _(tensor, v=v, pp=pp, ve3=ve3):
                w(tensor, "VE", ve3)
                if v >= 2:
                    w(tensor, "VE")
                for dc in range(DC):
                    tensor.transpose(psQt[:, pp * 1024 + dc * 128:pp * 1024 + (dc + 1) * 128],
                                     qib[:, dc * 128:(dc + 1) * 128], s.idb[:, :])
                inc(tensor.transpose(psRw[0:1, 0:128], s.sxcols[:, v:v + 1], s.idf[:, :]), "TE")
            te2 = cnt["TE"]
            @block.vector
            def _(vector, v=v, pp=pp, te2=te2):
                w(vector, "TE", te2)
                vector.tensor_copy(out=qT_v[:, :, v * 128:(v + 1) * 128],
                                   in_=psQt[:, pp * 1024:(pp + 1) * 1024]
                                   .rearrange("p (dc f) -> p dc f", f=128))
                inc(vector.tensor_copy(out=sxrow[0:1, v * 128:(v + 1) * 128],
                                       in_=psRw[0:1, 0:128]), "VE")
        veF = cnt["VE"]
        for qi, (q0, qn) in enumerate(QCH):
            @block.tensor
            def _(tensor, q0=q0, qn=qn, qi=qi, veF=veF):
                if qi == 0:
                    w(tensor, "VE", veF)
                else:
                    w(tensor, "SE")
                inc(tensor.matmul(psRw[:, 0:qn], s.onesf1[:, :], sxrow[0:1, q0:q0 + qn],
                                  start=True, stop=True), "TE")
            te3 = cnt["TE"]
            @block.scalar
            def _(scalar, q0=q0, qn=qn, te3=te3):
                w(scalar, "TE", te3)
                inc(scalar.copy(out=s.sxinvb[:, q0:q0 + qn], in_=psRw[:, 0:qn]), "SE")
        st0.close()

    def ph_g(self, g0_only=False):
        s, nc, block = self, self.nc, self.block
        cnt, w, inc, dinc = self.cnt, self.w, self.inc, self.dinc
        stg = contextlib.ExitStack()
        s.qT = self.sb("qT", [128, DC * ETK], BF16, stack=stg)
        s.sxinvb = self.sb("sxinvb", [128, ETK], F32, stack=stg)
        s.sxcols = self.sb("sxcols", [128, ENT + 1], F32, stack=stg)
        if self.debug == "phg0":
            self._dbg_un = nc.dram_tensor("d_un", [ETK, D], F32, kind="ExternalOutput").ap()
            self.dbg_outs.append("d_un")
        self.ph_g0(stg)
        if g0_only:
            d1 = nc.dram_tensor("d_qT", [128, DC * ETK], BF16, kind="ExternalOutput").ap()
            pass
            d2_ = nc.dram_tensor("d_sxinv", [128, ETK], F32, kind="ExternalOutput").ap()
            self.dbg_outs += ["d_qT", "d_sxinv"]
            self.dbg_store([(d1[:, :], s.qT[:, :]), (d2_[:, :], s.sxinvb[:, :])])
            stg.close()
            return
        w1s = self.sb("w1s", [128, 2 * 1024], BF16, stack=stg)
        w2s = self.sb("w2s", [128, 2 * 1024], BF16, stack=stg)
        hcf = self.sb("hcf", [128, 390], F32, stack=stg)
        cvA = self.sb("cvA", [128, 386], F32, stack=stg)
        h2sb = self.sb("h2sb", [128, 386], F32, stack=stg)
        h2f = self.sb("h2f", [128, NHC * 384], BF16, stack=stg)
        h2q = self.sb("h2q", [128, NHC * 384], BF16, stack=stg)
        mxc = self.sb("mxc", [128, 384], F32, stack=stg)
        s2b = self.sb("s2b", [128, 384], F32, stack=stg)
        u2 = self.sb("u2", [128, 768], F32, stack=stg)
        n2i = self.sb("n2i", [128, 768], I32, stack=stg)
        n2f = self.sb("n2f", [128, 768], F32, stack=stg)
        g2f = self.sb("g2f", [128, 768], F32, stack=stg)
        srow = self.sb("srow", [1, 390], F32, stack=stg)
        inv2c = self.sb("inv2c", [128, 6], F32, stack=stg)
        ffo = self.sb("ffo", [128, D], F32, stack=stg)
        dl3 = self.sb("dl3", [128, D], F32, stack=stg)
        x4t = self.sb("x4t", [128, D], F32, stack=stg)

        psH = self.psum("psH", [128, 1024], F32, stack=stg)
        teH_h = {}
        veH_h = {}
        teMo_h = {}
        mm2te_h = {}
        veQ_h = {}
        epi_ve_h = {}
        seS2b_h = {}

        def emit_mm1_conv_scale(si):
            t0, ntl = SEG[si]
            a = t0 * 128
            ntk = ntl * 128
            lo = max(a - 1, 0)
            hi = min(a + ntk + 1, ETK)
            ml = hi - lo
            pad_l = 1 if a == 0 else 0
            se_base = cnt["SE"]
            veH_l = [None] * NHC
            geC_l = [None] * NHC
            for hc in range(NHC):
                pp = hc % 2
                @block.sync
                def _(sync, si=si, hc=hc, pp=pp):
                    if si == 0 and hc == 0:
                        self.gate(sync)
                    k = si * NHC + hc
                    if k >= 2:
                        w(sync, "TE", teH_h[(si, hc - 2) if hc >= 2 else
                                            (si - 1, NHC + hc - 2)])
                    for dc in range(DC):
                        dinc(sync.dma_start(
                            out=w1s[:, pp * 1024 + dc * 128:pp * 1024 + (dc + 1) * 128],
                            in_=s.scr_w1[hc * DC + dc:hc * DC + dc + 1, :]
                            .rearrange("a (p f) -> (a p) f", p=128)), "LD")
                ldg = self.dsnap("LD")
                @block.tensor
                def _(tensor, si=si, hc=hc, pp=pp, lo=lo, ml=ml, ldg=ldg):
                    self.dwait(tensor, "LD", ldg)
                    k = si * NHC + hc
                    if k >= 2:
                        w(tensor, "VE", veH_h[(si, hc - 2) if hc >= 2 else
                                              (si - 1, NHC + hc - 2)])
                    if hc <= 1 and si >= 1:
                        w(tensor, "SE", seS2b_h[si - 1])
                    bk = hc % 2
                    for dc in range(DC):
                        i = tensor.matmul(psH[:, bk * 512:bk * 512 + ml],
                                          w1s[:, pp * 1024 + dc * 128:pp * 1024 + (dc + 1) * 128],
                                          s.qT[:, dc * ETK + lo:dc * ETK + lo + ml],
                                          start=(dc == 0), stop=(dc == DC - 1))
                    inc(i, "TE")
                teH_h[(si, hc)] = cnt["TE"]
                teH = cnt["TE"]
                @block.vector
                def _(vector, si=si, hc=hc, lo=lo, ml=ml, ntk=ntk, pad_l=pad_l, teH=teH):
                    w(vector, "TE", teH)
                    if si == 0 and hc == 0:
                        w(vector, "SE")
                    bk = hc % 2
                    if pad_l:
                        vector.memset(hcf[:, 0:1], 0.0)
                    if pad_l + ml < ntk + 2:
                        vector.memset(hcf[:, pad_l + ml:ntk + 2], 0.0)
                    inc(vector.scalar_tensor_tensor(out=hcf[:, pad_l:pad_l + ml],
                                                    in0=psH[:, bk * 512:bk * 512 + ml],
                                                    scalar=s.wneg[:, 0:1],
                                                    in1=s.sxinvb[:, lo:lo + ml],
                                                    op0=ALU.mult, op1=ALU.mult), "VE")
                veH_h[(si, hc)] = cnt["VE"]
                veH_l[hc] = cnt["VE"]
                seC = se_base + 2 * hc + 1
                @block.vector
                def _(vector, hc=hc, ntk=ntk, seC=seC):
                    w(vector, "SE", seC)
                    inc(vector.scalar_tensor_tensor(out=cvA[:, 0:ntk], in0=hcf[:, 0:ntk],
                                                    scalar=s.cw_g[:, hc:hc + 1],
                                                    in1=cvA[:, 0:ntk],
                                                    op0=ALU.mult, op1=ALU.add), "VE")
                @block.vector
                def _(vector, hc=hc, ntk=ntk):
                    inc(vector.scalar_tensor_tensor(out=cvA[:, 0:ntk], in0=hcf[:, 2:2 + ntk],
                                                    scalar=s.cw_g[:, 2 * NHC + hc:2 * NHC + hc + 1],
                                                    in1=cvA[:, 0:ntk],
                                                    op0=ALU.mult, op1=ALU.add), "VE")
                geC_l[hc] = cnt["VE"]
                seS = se_base + 2 * hc + 2
                @block.vector
                def _(vector, hc=hc, ntk=ntk, seS=seS):
                    w(vector, "SE", seS)
                    if hc == 0:
                        vector.scalar_tensor_tensor(out=mxc[:, 0:ntk], in0=h2sb[:, 0:ntk],
                                                    scalar=-1.0, in1=h2sb[:, 0:ntk],
                                                    op0=ALU.mult, op1=ALU.max)
                    else:
                        vector.scalar_tensor_tensor(out=mxc[:, 0:ntk], in0=h2sb[:, 0:ntk],
                                                    scalar=-1.0, in1=mxc[:, 0:ntk],
                                                    op0=ALU.mult, op1=ALU.max)
                        vector.tensor_tensor(out=mxc[:, 0:ntk], in0=mxc[:, 0:ntk],
                                             in1=h2sb[:, 0:ntk], op=ALU.max)
                    inc(vector.tensor_copy(out=h2f[:, hc * 384:hc * 384 + ntk],
                                           in_=h2sb[:, 0:ntk]), "VE")
            @block.scalar
            def _(scalar, ntk=ntk, veH_l=tuple(veH_l), geC_l=tuple(geC_l)):
                for hc in range(NHC):
                    w(scalar, "VE", veH_l[hc])
                    inc(scalar.activation(out=cvA[:, 0:ntk], in_=hcf[:, 1:1 + ntk],
                                          func=AF.Identity,
                                          scale=s.cw_g[:, NHC + hc:NHC + hc + 1],
                                          bias=s.cb_g[:, hc:hc + 1]), "SE")
                    w(scalar, "VE", geC_l[hc])
                    inc(scalar.activation(out=h2sb[:, 0:ntk], in_=cvA[:, 0:ntk],
                                          func=AF.Silu), "SE")
            assert cnt["SE"] == se_base + 2 * NHC
            veM = cnt["VE"]
            iv = (si % 2) * 3
            @block.tensor
            def _(tensor, ntl=ntl, veM=veM):
                w(tensor, "VE", veM)
                for k in range(ntl):
                    i = tensor.transpose(psH[:, k * 128:(k + 1) * 128],
                                         mxc[:, k * 128:(k + 1) * 128], s.idf[:, :])
                inc(i, "TE")
            teM = cnt["TE"]
            @block.vector
            def _(vector, ntl=ntl, teM=teM, iv=iv):
                w(vector, "TE", teM)
                for k in range(ntl):
                    self.chain(vector.reduce_max(out=s.sA[:, 8 + k:9 + k],
                                                 in_=psH[:, k * 128:(k + 1) * 128], axis=AX.X),
                               vector, "VE")
                    self.chain(vector.tensor_scalar(out=inv2c[:, iv + k:iv + k + 1],
                                                    in0=s.sA[:, 8 + k:9 + k],
                                                    scalar1=1e-5, scalar2=1.0 / 127.0,
                                                    op0=ALU.max, op1=ALU.mult), vector, "VE")
                    i = vector.reciprocal(out=s.sA[:, 12 + k:13 + k],
                                          in_=inv2c[:, iv + k:iv + k + 1])
                inc(i, "VE")
            veS2 = cnt["VE"]
            @block.tensor
            def _(tensor, ntl=ntl, veS2=veS2):
                w(tensor, "VE", veS2)
                for k in range(ntl):
                    i = tensor.transpose(psH[0:1, 512 + k * 128:512 + (k + 1) * 128],
                                         s.sA[:, 12 + k:13 + k], s.idf[:, :])
                inc(i, "TE")
            teS2 = cnt["TE"]
            @block.vector
            def _(vector, ntl=ntl, teS2=teS2):
                w(vector, "TE", teS2)
                inc(vector.tensor_copy(out=srow[0:1, 0:ntl * 128],
                                       in_=psH[0:1, 512:512 + ntl * 128]), "VE")
            veS3 = cnt["VE"]
            @block.tensor
            def _(tensor, ntk=ntk, veS3=veS3):
                w(tensor, "VE", veS3)
                inc(tensor.matmul(psH[:, 0:ntk], s.onesf1[:, :], srow[0:1, 0:ntk],
                                  start=True, stop=True), "TE")
            teS3 = cnt["TE"]
            @block.scalar
            def _(scalar, ntk=ntk, teS3=teS3):
                w(scalar, "TE", teS3)
                inc(scalar.copy(out=s2b[:, 0:ntk], in_=psH[:, 0:ntk]), "SE")
            seS2b_h[si] = cnt["SE"]

        def emit_quant(si):
            t0, ntl = SEG[si]
            ntk = ntl * 128
            h2f_v = h2f.rearrange("p (hc f) -> p hc f", f=384)
            h2q_v = h2q.rearrange("p (hc f) -> p hc f", f=384)
            for hp in range(NHC // 2):
                hc = hp * 2
                def emitq(e, hc=hc, ntk=ntk):
                    e.tensor_tensor(out=u2[:, :].rearrange("p (a f) -> p a f", a=2)[:, :, 0:ntk],
                                    in0=h2f_v[:, hc:hc + 2, 0:ntk],
                                    in1=s2b[:, 0:ntk].rearrange("p f -> p () f")
                                    .broadcast_to((128, 2, ntk)), op=ALU.mult)
                    e.tensor_scalar(out=u2[:, :], in0=u2[:, :],
                                    scalar1=128.0, scalar2=None, op0=ALU.add)
                    dst = h2q_v[:, hc:hc + 2, 0:ntk]
                    e.tensor_copy(out=n2i[:, :], in_=u2[:, :])
                    e.tensor_copy(out=n2f[:, :], in_=n2i[:, :])
                    e.tensor_tensor(out=g2f[:, :], in0=u2[:, :], in1=n2f[:, :],
                                    op=ALU.subtract)
                    nfv = n2f[:, :].rearrange("p (a f) -> p a f", a=2)[:, :, 0:ntk]
                    gv = g2f[:, :].rearrange("p (a f) -> p a f", a=2)[:, :, 0:ntk]
                    e.tensor_scalar(out=dst, in0=nfv, scalar1=-131.0, scalar2=None,
                                    op0=ALU.add)
                    for c in (-2.5, -1.5, -0.5, 0.5, 1.5, 2.5):
                        i = e.scalar_tensor_tensor(out=dst, in0=gv, scalar=c,
                                                   in1=dst, op0=ALU.is_ge, op1=ALU.add)
                    return i
                @block.vector
                def _(vector, si=si, hp=hp):
                    if hp == 0:
                        w(vector, "SE", seS2b_h[si])
                        if si >= 1:
                            w(vector, "TE", teMo_h[si - 1])
                    inc(emitq(vector), "VE")
            veQ_h[si] = cnt["VE"]

        def emit_mm2(si):
            t0, ntl = SEG[si]
            iv = (si % 2) * 3
            stg2 = contextlib.ExitStack()
            psMo = self.psum("psMo_%d" % si, [128, ntl * 1024], F32, stack=stg2)
            for hc in range(NHC):
                pp = hc % 2
                @block.sync
                def _(sync, si=si, hc=hc, pp=pp):
                    if si == 0 and hc == 0:
                        self.dwait(sync, "SD")
                    k = si * NHC + hc
                    if k >= 2:
                        w(sync, "TE", mm2te_h[(si, hc - 2) if hc >= 2 else
                                              (si - 1, NHC + hc - 2)])
                    dinc(sync.dma_start(out=w2s[:, pp * 1024:(pp + 1) * 1024],
                                       in_=s.scr_w2[hc * 128:(hc + 1) * 128, :]), "LD")
                ldg2 = self.dsnap("LD")
                @block.tensor
                def _(tensor, si=si, hc=hc, pp=pp, ntl=ntl, ldg2=ldg2):
                    self.dwait(tensor, "LD", ldg2)
                    if hc == 0:
                        w(tensor, "VE", veQ_h[si])
                        if si >= 1:
                            w(tensor, "VE", epi_ve_h[si - 1])
                    for k in range(ntl):
                        for hf2 in range(2):
                            i = tensor.matmul(
                                psMo[:, k * 1024 + hf2 * 512:k * 1024 + (hf2 + 1) * 512],
                                h2q[:, hc * 384 + k * 128:hc * 384 + k * 128 + 128],
                                w2s[:, pp * 1024 + hf2 * 512:pp * 1024 + (hf2 + 1) * 512],
                                start=(hc == 0), stop=(hc == NHC - 1))
                    inc(i, "TE")
                mm2te_h[(si, hc)] = cnt["TE"]
            teMo_h[si] = cnt["TE"]
            teMo = cnt["TE"]
            for k in range(ntl):
                v = t0 + k
                @block.vector
                def _(vector, k=k, teMo=teMo, iv=iv):
                    w(vector, "TE", teMo)
                    inc(vector.tensor_scalar(out=ffo[:, :],
                                             in0=psMo[:, k * 1024:(k + 1) * 1024],
                                             scalar1=inv2c[:, iv + k:iv + k + 1],
                                             scalar2=s.wneg[:, 1:2],
                                             op0=ALU.mult, op1=ALU.mult), "VE")
                vex = cnt["VE"]
                @block.vector
                def _(vector, vex=vex):
                    w(vector, "VE", vex)
                    inc(self.stats_v(vector, ffo[:, 0:512], ffo[:, 512:1024], 0), "VE")
                ve1 = cnt["VE"]
                @block.scalar
                def _(scalar, ve1=ve1):
                    w(scalar, "VE", ve1)
                    inc(self.stats_s(scalar, 0), "SE")
                se1 = cnt["SE"]
                @block.vector
                def _(vector, v=v, se1=se1):
                    w(vector, "SE", se1)
                    self.dwait(vector, "ST")
                    vector.tensor_scalar(out=dl3[:, :], in0=ffo[:, :], scalar1=s.sA[:, 0:1],
                                         scalar2=s.sA[:, 1:2], op0=ALU.subtract, op1=ALU.mult)
                    inc(vector.tensor_tensor(out=x4t[:, :],
                                             in0=s.x2win[:, v * D:(v + 1) * D],
                                             in1=dl3[:, :], op=ALU.add), "VE")
                vey = cnt["VE"]
                @block.gpsimd
                def _(g, v=v, vey=vey):
                    w(g, "VE", vey)
                    dinc(g.dma_start(out=s.y[v * 128:(v + 1) * 128, :],
                                    in_=x4t[:, :]), "ST")
            epi_ve_h[si] = cnt["VE"]
            stg2.close()

        for si in range(len(SEG)):
            emit_mm1_conv_scale(si)
            if si >= 1:
                emit_mm2(si - 1)
            emit_quant(si)
        emit_mm2(len(SEG) - 1)
        stg.close()

    def fin(self):
        for st in reversed(self._open_stacks):
            st.close()
        block = self.block
        w = self.w
        @block.sync
        def _(sync):
            for nm in ("VE", "SE", "TE", "GE"):
                w(sync, nm)
            for nm in ("ST", "GD", "SD", "LD"):
                self.dwait(sync, nm)
        @block.gpsimd
        def _(g):
            self.dwait(g, "ST")
        self.stack.close()
        return self

import numpy as np
from concourse.bass_utils import run_bass_kernel_spmd

_CACHED = {}


def _build_cached():
    if "m" not in _CACHED:
        _CACHED["m"] = MK(debug=None).build()
    return _CACHED["m"]


def _host_inputs(inp, b, j):
    f32 = np.float32
    d = {}
    d["xb"] = np.ascontiguousarray(inp["L0_attn_x"][b]).astype(f32)
    d["jsel"] = np.array([[j, 1 - j]], dtype=f32)
    d["identf"] = np.eye(128, dtype=f32)
    import ml_dtypes
    d["identb"] = np.eye(128).astype(ml_dtypes.bfloat16)
    d["WiT"] = np.ascontiguousarray(inp["L0_attn_Wi"].T)
    d["WrT"] = np.ascontiguousarray(inp["L0_attn_Wr"].T)
    d["bi_pp"] = np.ascontiguousarray(inp["L0_attn_bi"].reshape(DC, 128).T)
    d["br_pp"] = np.ascontiguousarray(inp["L0_attn_br"].reshape(DC, 128).T)
    d["lam"] = np.asarray(inp["L0_attn_cap_lambda"]).reshape(1, 1)
    d["Win"] = np.asarray(inp["L0_ffn_Win"])
    d["binr"] = np.asarray(inp["L0_ffn_bin_"]).reshape(1, 64)
    d["W1r"] = np.ascontiguousarray(np.asarray(inp["L0_ffn_W1"]).reshape(1, 1024))
    d["b1r"] = np.ascontiguousarray(np.asarray(inp["L0_ffn_b1"]).reshape(1, 1024))
    d["W2r"] = np.ascontiguousarray(np.asarray(inp["L0_ffn_W2"]).reshape(1, 1024))
    d["b2r"] = np.asarray(inp["L0_ffn_b2"]).reshape(1, 64)
    d["Wout"] = np.asarray(inp["L0_ffn_Wout"])
    d["boutr"] = np.asarray(inp["L0_ffn_bout"]).reshape(1, D)
    d["Wq"] = np.asarray(inp["L1_attn_Wq"])
    d["Wk_up"] = np.asarray(inp["L1_attn_Wk_up"])
    d["Wv_up"] = np.asarray(inp["L1_attn_Wv_up"])
    d["WoT"] = np.ascontiguousarray(np.asarray(inp["L1_attn_Wo"]).T)
    d["WkvdT"] = np.ascontiguousarray(np.asarray(inp["L1_attn_Wkv_down"]).T)
    d["w1T"] = np.ascontiguousarray(np.asarray(inp["L1_ffn_w1"]).T)
    d["w2T"] = np.ascontiguousarray(np.asarray(inp["L1_ffn_w2"]).T)
    cw = np.asarray(inp["L1_ffn_conv_w"])[:, 0, :]
    d["cwr"] = np.ascontiguousarray(
        np.concatenate([cw[:, k].reshape(NHC, 128).T for k in range(3)], axis=1))
    d["cbr"] = np.ascontiguousarray(np.asarray(inp["L1_ffn_conv_b"]).reshape(NHC, 128).T)
    return {k: np.ascontiguousarray(np.asarray(v, dtype=np.float32)
                                    if v.dtype == np.float64 else v)
            for k, v in d.items()}


def kernel(**inputs):
    m = _build_cached()
    ins = [_host_inputs(inputs, c // 2, c % 2) for c in range(8)]
    res = run_bass_kernel_spmd(m.nc, ins, core_ids=list(range(8)))
    out = np.empty((4, T, D), dtype=np.float32)
    for c in range(8):
        b, j = c // 2, c % 2
        yv = np.asarray(res.results[c]["y"])
        off = 0 if j == 0 else 128
        out[b, j * 2048:(j + 1) * 2048] = yv[off:off + 2048]
    return out
